# version 30
# baseline (speedup 1.0000x reference)
import numpy as np
import concourse.bass as bass
import concourse.mybir as mybir
from concourse.bass_utils import run_bass_kernel_spmd
from contextlib import ExitStack

F32 = mybir.dt.float32
BF16 = mybir.dt.bfloat16
AF = mybir.ActivationFunctionType
ALU = mybir.AluOpType
AX = mybir.AxisListType

DM = 1024
DI = 2048
NST = 16
DTR = 64
TC = 512
NTB = TC // 128
NDM = DM // 128
NDI = DI // 128
JB = 4
LN_EPS = 1e-5


class _Buf:
    __slots__ = ("writer", "readers")

    def __init__(self):
        self.writer = None
        self.readers = []


class _Op:
    __slots__ = ("eng", "emit", "deps", "eng_pos", "sem_val", "slot", "needs_inc")


class _Slot:
    __slots__ = ("name", "count", "sem")

    def __init__(self, name):
        self.name = name
        self.count = 0
        self.sem = None


class Plan:
    ENGINES = ("sync", "tensor", "scalar", "vector")

    def __init__(self):
        self.eng_ops = {e: [] for e in self.ENGINES}
        self.all_ops = []
        self.slots = []

    def slot(self, name):
        s = _Slot(name)
        self.slots.append(s)
        return s

    def add(self, eng, emit, reads=(), writes=(), slot=None):
        op = _Op()
        op.eng = eng
        op.emit = emit
        op.slot = slot
        op.needs_inc = False
        op.sem_val = None
        deps = set()
        for b in reads:
            if b.writer is not None:
                deps.add(b.writer)
        for b in writes:
            for r in b.readers:
                deps.add(r)
            if b.writer is not None:
                deps.add(b.writer)
        deps.discard(op)
        op.deps = deps
        op.eng_pos = len(self.eng_ops[eng])
        self.eng_ops[eng].append(op)
        self.all_ops.append(op)
        if slot is not None:
            op.sem_val = slot.count + 16
            slot.count += 16
        for b in reads:
            b.readers.append(op)
        for b in writes:
            b.writer = op
            b.readers = []
        return op

    def finalize(self, nc, stack):
        for op in self.all_ops:
            for d in op.deps:
                if d.slot is None:
                    d.needs_inc = True
        eng_sem = {e: stack.enter_context(nc.semaphore("sem_" + e))
                   for e in self.ENGINES}
        for s in self.slots:
            s.sem = stack.enter_context(nc.semaphore("dma_" + s.name))
        counts = {e: 0 for e in self.ENGINES}
        for e in self.ENGINES:
            for op in self.eng_ops[e]:
                if op.slot is None and op.needs_inc:
                    counts[e] += 1
                    op.sem_val = counts[e]
        self.eng_sem = eng_sem

    def emit_engine(self, eng_name, eng_obj):
        watermark = {}
        for op in self.eng_ops[eng_name]:
            waits = {}
            if op.slot is not None and op.sem_val > 16:
                waits[id(op.slot.sem)] = (op.slot.sem, op.sem_val - 16)
            for d in op.deps:
                sem = d.slot.sem if d.slot is not None else self.eng_sem[d.eng]
                k = id(sem)
                if k not in waits or waits[k][1] < d.sem_val:
                    waits[k] = (sem, d.sem_val)
            for sem, val in waits.values():
                k = id(sem)
                if watermark.get(k, 0) < val:
                    eng_obj.wait_ge(sem, val)
                    watermark[k] = val
            ins = op.emit(eng_obj)
            if op.slot is not None:
                ins.then_inc(op.slot.sem, 16)
            elif op.needs_inc:
                ins.then_inc(self.eng_sem[op.eng], 1)


def build_nc(L):
    NCH = L // TC
    nc = bass.Bass("TRN2", target_bir_lowering=False)
    p = Plan()

    d_x = nc.dram_tensor("x", [L, DM], F32, kind="ExternalInput")
    d_WaT = nc.dram_tensor("WaT", [DM, 128], F32, kind="ExternalInput")
    d_CaT = nc.dram_tensor("CaT", [128, DM], F32, kind="ExternalInput")
    d_la = nc.dram_tensor("la", [128, 1], F32, kind="ExternalInput")
    d_Da = nc.dram_tensor("Da", [128, NDM], F32, kind="ExternalInput")
    d_Win = nc.dram_tensor("Win", [DM, 2 * DI], F32, kind="ExternalInput")
    d_cw = nc.dram_tensor("cw", [128, 4 * NDI], F32, kind="ExternalInput")
    d_cb = nc.dram_tensor("cb", [128, NDI], F32, kind="ExternalInput")
    d_Wx = nc.dram_tensor("Wx", [DI, 96], F32, kind="ExternalInput")
    d_Wdt = nc.dram_tensor("Wdt", [DTR, DI], F32, kind="ExternalInput")
    d_bdt = nc.dram_tensor("bdt", [128, NDI], F32, kind="ExternalInput")
    d_Alog = nc.dram_tensor("Alog", [128, NDI * NST], F32, kind="ExternalInput")
    d_Db = nc.dram_tensor("Db", [128, NDI], F32, kind="ExternalInput")
    d_Wout = nc.dram_tensor("Wout", [DI, DM], F32, kind="ExternalInput")
    d_WfT = nc.dram_tensor("WfT", [DM, DM], F32, kind="ExternalInput")
    d_ident = nc.dram_tensor("ident", [128, 128], F32, kind="ExternalInput")
    d_onesred = nc.dram_tensor("onesred", [128, 2], F32, kind="ExternalInput")
    d_onesrow = nc.dram_tensor("onesrow", [1, 128], F32, kind="ExternalInput")
    d_y = nc.dram_tensor("y", [L, DM], F32, kind="ExternalOutput")
    d_wint = nc.dram_tensor("wint", [NDM, 128, 2 * NDI, 128], BF16)
    d_woutt = nc.dram_tensor("woutt", [NDI, 128, NDM, 128], BF16)
    d_wft = nc.dram_tensor("wft", [NDM, 128, NDM, 128], BF16)
    d_diag = nc.dram_tensor("cdiag", [NDI, 128, 4 * 128], BF16)
    d_bcr = nc.dram_tensor("bcr", [1, 32, TC], F32)
    d_srow = nc.dram_tensor("srow", [1, 4], F32)

    stack = ExitStack()
    sb = lambda name, shape, dt: stack.enter_context(nc.sbuf_tensor(name, shape, dt))
    psum = lambda name: stack.enter_context(nc.psum_tensor(name, [128, 512], F32))

    s_wa = sb("s_wa", [128, NDM * 128], BF16)
    s_ca = sb("s_ca", [128, DM], BF16)
    s_wx = sb("s_wx", [128, NDI * 96], F32)
    s_wdt = sb("s_wdt", [DTR, DI], BF16)
    s_win = [sb("s_win%d" % i, [128, NDI * 128], BF16) for i in range(2)]
    s_wo2 = [sb("s_wo2%d" % i, [128, JB * 128], BF16) for i in range(2)]
    s_wfp = [sb("s_wfp%d" % i, [128, NDM * 128], BF16) for i in range(2)]
    s_Aneg = sb("s_Aneg", [128, NDI * NST], F32)
    s_Aa = sb("s_Aa", [128, 1], F32)
    s_Aabc = sb("s_Aabc", [128, TC], F32)
    s_Da = sb("s_Da", [128, NDM], F32)
    s_Db = sb("s_Db", [128, NDI], F32)
    s_cw = sb("s_cw", [128, 4 * NDI], F32)
    s_cb = sb("s_cb", [128, NDI], F32)
    s_cbh = sb("s_cbh", [128, NDI], F32)
    s_bdth = sb("s_bdth", [128, NDI], F32)
    s_ident = sb("s_ident", [128, 128], F32)
    s_onesred = sb("s_onesred", [128, 2], F32)
    s_onesrow = sb("s_onesrow", [1, 128], F32)
    s_thr = sb("s_thr", [128, 3], F32)
    s_konst = sb("s_konst", [128, 5], F32)

    s_tst = [sb("s_tst0", [128, DM], F32)]
    s_x = [sb("s_x%d" % q, [128, NDM * TC], F32) for q in range(2)]
    s_xln = [sb("s_xln%d" % q, [128, NDM * TC], BF16) for q in range(2)]
    s_mrow = sb("s_mrow", [1, TC], F32)
    s_varrow = sb("s_varrow", [1, TC], F32)
    s_xbs = [sb("s_xbs%d" % i, [128, TC + 3], BF16) for i in range(2)]
    s_cacc = [sb("s_cacc%d" % i, [128, TC], F32) for i in range(2)]
    s_halo = sb("s_halo", [128, NDI * 3], BF16)
    s_dg = [sb("s_dg%d" % i, [128, 4 * 128], BF16) for i in range(2)]
    s_cacc2 = [sb("s_cacc2%d" % i, [128, TC], F32) for i in range(2)]
    s_xc = [sb("s_xc0", [128, NDI * TC], F32)]
    s_dlow = [sb("s_dlow0", [DTR, TC], BF16)]
    s_bc = [sb("s_bc0", [32, TC], F32)]
    s_Bbc = [sb("s_Bbc%d" % i, [128, TC], F32) for i in range(2)]
    s_Cbc = [sb("s_Cbc%d" % i, [128, TC], F32) for i in range(2)]
    s_dt = [sb("s_dt%d" % i, [128, TC], F32) for i in range(JB)]
    s_v = [sb("s_v%d" % i, [128, TC], F32) for i in range(JB)]
    s_da = [sb("s_da%d" % i, [128, TC], F32) for i in range(2)]
    s_u = [sb("s_u%d" % i, [128, TC], F32) for i in range(2)]
    s_h = [sb("s_h%d" % i, [128, TC], F32) for i in range(2)]
    s_zt = [sb("s_zt%d" % i, [128, TC], F32) for i in range(2)]
    s_Hcar = sb("s_Hcar", [128, NDI * NST], F32)
    s_y = [sb("s_y%d" % i, [128, TC], F32) for i in range(JB)]
    s_yg = [sb("s_yg0", [128, NDI * TC], BF16)]
    s_ha = [sb("s_ha%d" % q, [128, TC], F32) for q in range(2)]
    s_habf = sb("s_habf", [128, TC], BF16)
    s_small = sb("s_small", [128, 8], F32)
    s_srow = sb("s_srow", [1, 4], F32)

    ps_a, ps_b = psum("ps_a"), psum("ps_b")
    ps_xd, ps_dt = psum("ps_xd"), psum("ps_dt")
    ps_m, ps_q = psum("ps_m"), psum("ps_q")
    ps_rbc, ps_mrbc = psum("ps_rbc"), psum("ps_mrbc")

    B = _Buf
    b_wa, b_ca, b_wx, b_wdt = B(), B(), B(), B()
    b_wfp = [B(), B()]
    b_wft = [B() for _ in range(NDM)]
    b_win = [B(), B()]
    b_wo2 = [B(), B()]
    b_Aneg, b_Aa, b_Aabc, b_Da, b_Db = B(), B(), B(), B(), B()
    b_cw, b_cb, b_cbh, b_bdth = B(), B(), B(), B()
    b_ident, b_onesred, b_onesrow, b_thr, b_konst = B(), B(), B(), B(), B()
    b_tst = [B(), B()]
    b_x = [[B() for _ in range(NDM)] for _ in range(2)]
    b_xln = [[B() for _ in range(NDM)] for _ in range(2)]
    b_mrow, b_varrow, b_rrow = B(), B(), B()
    b_xbs = [B(), B()]
    b_dg = [B(), B()]
    b_diag = B()
    b_cacc2 = [B(), B()]
    b_cacc = [B(), B()]
    b_halo = [B() for _ in range(NDI)]
    b_xc = [[B() for _ in range(NDI)]]
    b_dlow = [B()]
    b_bc = [B()]
    b_bcr = [B()]
    b_Bbc = [B(), B()]
    b_Cbc = [B(), B()]
    b_dt = [B() for _ in range(JB)]
    b_v = [B() for _ in range(JB)]
    b_da = [B(), B()]
    b_u = [B(), B()]
    b_h = [B(), B()]
    b_zt = [B(), B()]
    b_Hcar = B()
    b_y = [B() for _ in range(JB)]
    b_yg = [[B() for _ in range(NDI)]]
    b_ha = [B(), B()]
    b_habf = B()
    b_small, b_srow = B(), B()
    b_psa, b_psb, b_psxd, b_psdt = B(), B(), B(), B()
    b_psm, b_psq, b_psrbc, b_psmrbc = B(), B(), B(), B()
    b_wint = [B() for _ in range(2 * NDI)]
    b_woutt = [B() for _ in range(NDM)]

    sl_init = p.slot("init")
    sl_wstage = p.slot("wstage")
    sl_dg = p.slot("dg")
    sl_ws = [p.slot("ws%d" % i) for i in range(4)]
    sl_wt = [p.slot("wt%d" % i) for i in range(2)]
    sl_wtile = p.slot("wtile")
    sl_xin = [p.slot("xin%d" % k) for k in range(2)]
    sl_xout = [p.slot("xout%d" % k) for k in range(2)]
    sl_win = [p.slot("win%d" % i) for i in range(2)]
    sl_wo2 = [p.slot("wo2%d" % i) for i in range(2)]
    sl_wfp = [p.slot("wfp%d" % i) for i in range(2)]
    sl_bbc = [p.slot("bbc%d" % i) for i in range(2)]
    sl_cbc = [p.slot("cbc%d" % i) for i in range(2)]
    sl_bcr = [p.slot("bcr0")]
    sl_thr = p.slot("thr")

    _rot = [(ps_a, b_psa), (ps_b, b_psb)]
    _rot_i = [0]

    def rot():
        t = _rot[_rot_i[0] % 2]
        _rot_i[0] += 1
        return t

    def dma(dst_ap, src_ap, reads, writes, slot):
        return p.add("sync",
                     (lambda d=dst_ap, s=src_ap: (lambda e: e.dma_start(out=d, in_=s)))(),
                     reads=reads, writes=writes, slot=slot)

    dma(s_ident[:, :], d_ident[:, :], [], [b_ident], sl_init)
    dma(s_onesred[:, :], d_onesred[:, :], [], [b_onesred], sl_init)
    dma(s_onesrow[:, :], d_onesrow[:, :], [], [b_onesrow], sl_init)
    dma(s_Da[:, :], d_Da[:, :], [], [b_Da], sl_init)
    dma(s_Db[:, :], d_Db[:, :], [], [b_Db], sl_init)
    dma(s_cw[:, :], d_cw[:, :], [], [b_cw], sl_init)
    dma(s_cb[:, :], d_cb[:, :], [], [b_cb], sl_init)
    dma(s_cbh[:, :], d_cb[:, :], [], [b_cbh], sl_init)
    dma(s_bdth[:, :], d_bdt[:, :], [], [b_bdth], sl_init)
    dma(s_wx[:, :].rearrange("p (k m) -> p k m", k=NDI),
        d_Wx.rearrange("(k p) m -> p k m", p=128), [], [b_wx], sl_init)
    dma(s_Aneg[:, :], d_Alog[:, :], [], [b_Aneg], sl_init)
    dma(s_Aa[:, :], d_la[:, :], [], [b_Aa], sl_init)

    p.add("vector", lambda e: e.tensor_scalar_mul(s_cbh[:, :], s_cbh[:, :], 0.5),
          reads=[b_cbh], writes=[b_cbh])
    p.add("vector", lambda e: e.tensor_scalar_mul(s_bdth[:, :], s_bdth[:, :], 0.5),
          reads=[b_bdth], writes=[b_bdth])
    p.add("scalar", lambda e: e.activation(s_Aneg[:, :], s_Aneg[:, :], AF.Exp),
          reads=[b_Aneg], writes=[b_Aneg])
    p.add("vector", lambda e: e.tensor_scalar_mul(s_Aneg[:, :], s_Aneg[:, :], -1.0),
          reads=[b_Aneg], writes=[b_Aneg])
    p.add("scalar", lambda e: e.activation(s_Aa[:, :], s_Aa[:, :], AF.Exp),
          reads=[b_Aa], writes=[b_Aa])
    p.add("vector", lambda e: e.memset(s_Aabc[:, :], 1.0), reads=[], writes=[b_Aabc])
    p.add("vector", lambda e: e.tensor_scalar_mul(s_Aabc[:, :], s_Aabc[:, :], s_Aa[:, 0:1]),
          reads=[b_Aa, b_Aabc], writes=[b_Aabc])
    p.add("vector", lambda e: e.memset(s_halo[:, :], 0.0), reads=[], writes=b_halo)
    p.add("vector", lambda e: e.memset(s_konst[:, 0:1], LN_EPS), reads=[b_konst], writes=[b_konst])
    p.add("vector", lambda e: e.memset(s_konst[:, 1:2], -0.5), reads=[b_konst], writes=[b_konst])
    p.add("vector", lambda e: e.memset(s_konst[:, 2:3], 0.5), reads=[b_konst], writes=[b_konst])
    p.add("vector", lambda e: e.memset(s_konst[:, 3:4], 0.0495), reads=[b_konst], writes=[b_konst])
    p.add("vector", lambda e: e.memset(s_konst[:, 4:5], 0.0505), reads=[b_konst], writes=[b_konst])

    stage = s_xc[0]
    stage2 = s_yg[0]
    HALFC = NDI * TC // 2
    stg = lambda h, n: stage[:, HALFC * h:HALFC * h + n]
    stg2 = lambda h, n: stage2[:, HALFC * h:HALFC * h + n]
    bstg = [b_xc[0][0:8], b_xc[0][8:16]]
    bstg2 = [b_yg[0][0:8], b_yg[0][8:16]]
    QC = NDI * TC // 4
    stq = lambda h, n: stage[:, QC * h:QC * h + n]
    stq2 = lambda h, n: stage2[:, QC * h:QC * h + n]
    bstq = [b_xc[0][4 * i:4 * i + 4] for i in range(4)]
    bstq2 = [b_yg[0][4 * i:4 * i + 4] for i in range(4)]
    bstage = b_xc[0]
    bstage2 = b_yg[0]

    dma(stage[:, 0:NDM * 128].rearrange("p (k m) -> p k m", m=128),
        d_WaT.rearrange("(k p) m -> p k m", p=128), bstage, bstage, sl_wstage)
    p.add("vector", lambda e: e.tensor_copy(s_wa[:, :], stage[:, 0:NDM * 128]),
          reads=bstage, writes=[b_wa])
    dma(stage[:, 0:DM], d_CaT[:, :], bstage, bstage, sl_wstage)
    p.add("vector", lambda e: e.tensor_copy(s_ca[:, :], stage[:, 0:DM]),
          reads=bstage, writes=[b_ca])
    dma(stage[0:DTR, 0:DI], d_Wdt[:, :], bstage, bstage, sl_wstage)
    p.add("vector", lambda e: e.tensor_copy(s_wdt[:, :], stage[0:DTR, 0:DI]),
          reads=bstage, writes=[b_wdt])
    for k in range(NDM):
        h = k % 4
        dma(stq(h, DM), d_WfT[128 * k:128 * (k + 1), :], bstq[h], bstq[h], sl_ws[k % 4])
        p.add("vector", (lambda k=k, h=h: (lambda e: e.tensor_reduce(
            s_small[:, k:k + 1], stq(h, DM),
            AX.X, ALU.add, apply_absolute_value=True)))(),
            reads=bstq[h], writes=[b_small])
    p.add("tensor", lambda e: e.matmul(ps_a[0:1, 0:8], s_onesred[:, 1:2], s_small[:, :],
                                       start=True, stop=True),
          reads=[b_onesred, b_small], writes=[b_psa])
    p.add("vector", lambda e: e.reduce_sum(s_srow[0:1, 0:1], ps_a[0:1, 0:8], AX.X),
          reads=[b_psa], writes=[b_srow])
    p.add("vector", lambda e: e.tensor_scalar(s_srow[0:1, 0:1], s_srow[0:1, 0:1],
                                              1.0 / (DM * DM), 1e-5, ALU.mult, ALU.max),
          reads=[b_srow], writes=[b_srow])
    p.add("vector", lambda e: e.tensor_scalar_mul(s_srow[0:1, 1:2], s_srow[0:1, 0:1], 0.5),
          reads=[b_srow], writes=[b_srow])
    p.add("vector", lambda e: e.tensor_scalar_mul(s_srow[0:1, 2:3], s_srow[0:1, 0:1], -0.5),
          reads=[b_srow], writes=[b_srow])
    dma(d_srow[0:1, 0:3], s_srow[0:1, 0:3], [b_srow], [b_srow], sl_thr)
    dma(s_thr[:, :], bass.AP(d_srow, 0, [[0, 128], [1, 3]]), [b_srow], [b_thr], sl_thr)
    for k in range(NDM):
        h = k % 4
        dma(stq(h, DM), d_WfT[128 * k:128 * (k + 1), :], bstq[h], bstq[h], sl_ws[k % 4])
        for qq in range(2):
            lo = TC * qq
            p.add("vector", (lambda lo=lo, h=h: (lambda e: e.tensor_single_scalar(
                s_cacc[0][:, :], stq(h, DM)[:, lo:lo + TC], s_thr[:, 1:2], ALU.is_gt)))(),
                reads=bstq[h] + [b_thr], writes=[b_cacc[0]])
            p.add("vector", (lambda lo=lo, h=h: (lambda e: e.tensor_single_scalar(
                s_cacc[1][:, :], stq(h, DM)[:, lo:lo + TC], s_thr[:, 2:3], ALU.is_lt)))(),
                reads=bstq[h] + [b_thr], writes=[b_cacc[1]])
            p.add("vector", (lambda lo=lo, h=h: (lambda e: e.tensor_sub(
                stq2(h, DM)[:, lo:lo + TC], s_cacc[0][:, :], s_cacc[1][:, :])))(),
                reads=[b_cacc[0], b_cacc[1]], writes=bstq2[h])
        dma(d_wft[k], stq2(h, DM).rearrange("p (f m) -> p f m", f=NDM),
            bstq2[h], b_wft, sl_wt[k % 2])

    def init_wio():
        for k in range(NDM):
            h = k % 2
            dma(stg(h, 2 * DI), d_Win[128 * k:128 * (k + 1), :],
                bstg[h], bstg[h], sl_ws[k % 4])
            p.add("vector", (lambda h=h: (lambda e: e.tensor_copy(
                stg2(h, 2 * DI), stg(h, 2 * DI))))(),
                reads=bstg[h], writes=bstg2[h])
            dma(d_wint[k], stg2(h, 2 * DI).rearrange("p (j f) -> p j f", j=2 * NDI),
                bstg2[h], b_wint, sl_wt[k % 2])
            yield
        for k in range(NDI):
            h = k % 4
            dma(stq(h, DM), d_Wout[128 * k:128 * (k + 1), :],
                bstq[h], bstq[h], sl_ws[k % 4])
            p.add("vector", (lambda h=h: (lambda e: e.tensor_copy(
                stq2(h, DM), stq(h, DM))))(),
                reads=bstq[h], writes=bstq2[h])
            dma(d_woutt[k], stq2(h, DM).rearrange("p (f m) -> p f m", f=NDM),
                bstq2[h], b_woutt, sl_wt[k % 2])
            yield
        for j in range(NDI):
            h = j % 4
            for k in range(4):
                p.add("vector", (lambda j=j, k=k, h=h: (lambda e: e.tensor_single_scalar(
                    stq(h, TC)[:, 128 * k:128 * (k + 1)], s_ident[:, :],
                    s_cw[:, 4 * j + k:4 * j + k + 1], ALU.mult)))(),
                    reads=[b_ident, b_cw] + bstq[h], writes=bstq[h])
            p.add("vector", (lambda h=h: (lambda e: e.tensor_copy(
                stq2(h, TC), stq(h, TC))))(),
                reads=bstq[h], writes=bstq2[h])
            dma(d_diag[j], stq2(h, TC), bstq2[h], [b_diag], sl_wt[j % 2])
            yield

    def layernorm(q):
        sx, sxln = s_x[q], s_xln[q]
        bx, bxln = b_x[q], b_xln[q]
        for j in range(NDM):
            if j in (2, 4, 6):
                yield
            sq = s_cacc[j % 2]
            p.add("scalar", (lambda j=j, sq=sq, sx=sx: (lambda e: e.activation(
                sq[:, :], sx[:, TC * j:TC * (j + 1)], AF.Square)))(),
                reads=[bx[j]], writes=[b_cacc[j % 2]])
            p.add("tensor", (lambda j=j, sx=sx: (lambda e: e.matmul(
                ps_m[0:1, 0:TC], s_onesred[:, 0:1], sx[:, TC * j:TC * (j + 1)],
                start=(j == 0), stop=(j == NDM - 1), skip_group_check=True)))(),
                reads=[b_onesred, bx[j]], writes=[b_psm])
            p.add("tensor", (lambda j=j, sq=sq: (lambda e: e.matmul(
                ps_q[0:1, 0:TC], s_onesred[:, 0:1], sq[:, :],
                start=(j == 0), stop=(j == NDM - 1), skip_group_check=True)))(),
                reads=[b_onesred, b_cacc[j % 2]], writes=[b_psq])
        p.add("vector", lambda e: e.tensor_copy(s_mrow[0:1, :], ps_m[0:1, 0:TC]),
              reads=[b_psm], writes=[b_mrow])
        p.add("vector", lambda e: e.tensor_mul(s_varrow[0:1, :], s_mrow[0:1, :], s_mrow[0:1, :]),
              reads=[b_mrow], writes=[b_varrow])
        p.add("vector", lambda e: e.tensor_sub(s_varrow[0:1, :], ps_q[0:1, 0:TC], s_varrow[0:1, :]),
              reads=[b_psq, b_varrow], writes=[b_varrow])
        p.add("scalar", lambda e: e.activation(s_varrow[0:1, :], s_varrow[0:1, :], AF.Ln,
                                               bias=s_konst[0:1, 0:1]),
              reads=[b_varrow, b_konst], writes=[b_varrow])
        p.add("scalar", lambda e: e.activation(s_varrow[0:1, :], s_varrow[0:1, :], AF.Exp,
                                               scale=s_konst[0:1, 1:2]),
              reads=[b_varrow, b_konst], writes=[b_rrow])
        p.add("vector", lambda e: e.tensor_mul(s_mrow[0:1, :], s_mrow[0:1, :], s_varrow[0:1, :]),
              reads=[b_mrow, b_rrow], writes=[b_mrow])
        p.add("tensor", lambda e: e.matmul(ps_rbc[:, 0:TC], s_onesrow[:, :], s_varrow[0:1, :],
                                           start=True, stop=True),
              reads=[b_onesrow, b_rrow], writes=[b_psrbc])
        p.add("tensor", lambda e: e.matmul(ps_mrbc[:, 0:TC], s_onesrow[:, :], s_mrow[0:1, :],
                                           start=True, stop=True),
              reads=[b_onesrow, b_mrow], writes=[b_psmrbc])
        yield

        def ap_mul(j):
            sq = s_cacc[j % 2]
            p.add("vector", (lambda j=j, sq=sq, sx=sx: (lambda e: e.tensor_mul(
                sq[:, :], sx[:, TC * j:TC * (j + 1)], ps_rbc[:, 0:TC])))(),
                reads=[bx[j], b_psrbc], writes=[b_cacc[j % 2]])

        def ap_sub(j):
            sq = s_cacc[j % 2]
            p.add("vector", (lambda j=j, sq=sq, sxln=sxln: (lambda e: e.tensor_sub(
                sxln[:, TC * j:TC * (j + 1)], sq[:, :], ps_mrbc[:, 0:TC])))(),
                reads=[b_cacc[j % 2], b_psmrbc], writes=[bxln[j]])
        for j in range(0, NDM, 2):
            ap_mul(j); ap_mul(j + 1); ap_sub(j); ap_sub(j + 1)
            yield

    def ffn(q):
        sxln, bxln = s_xln[q], b_xln[q]
        sx, bx = s_x[q], b_x[q]
        banks2 = [(ps_xd, b_psxd), (ps_rbc, b_psrbc)]
        pend = []
        for j in range(NDM):
            w = j % 2
            dma(s_wfp[w][:, :].rearrange("p (k m) -> p k m", k=NDM),
                d_wft[:, :, j, :].transpose([1, 0, 2]),
                [b_wfp[w], b_wft[j]], [b_wfp[w]], sl_wfp[w])
            ps, bp = banks2[j % 2]
            for k in range(NDM):
                p.add("tensor", (lambda k=k, ps=ps, w=w, sxln=sxln: (lambda e: e.matmul(
                    ps[:, 0:TC], s_wfp[w][:, 128 * k:128 * (k + 1)],
                    sxln[:, TC * k:TC * (k + 1)],
                    start=(k == 0), stop=(k == NDM - 1))))(),
                    reads=[b_wfp[w], bxln[k]], writes=[bp])
            pend.append((lambda j=j, ps=ps, bp=bp: p.add(
                "vector", (lambda j=j, ps=ps, sx=sx: (lambda e: e.scalar_tensor_tensor(
                    sx[:, TC * j:TC * (j + 1)], ps[:, 0:TC], s_thr[:, 0:1],
                    sx[:, TC * j:TC * (j + 1)], ALU.mult, ALU.add)))(),
                reads=[bp, b_thr, bx[j]], writes=[bx[j]])))
            if len(pend) > 1:
                pend.pop(0)()
            yield
        while pend:
            pend.pop(0)()

    def front(c):
        q = c % 2
        t0 = c * TC
        sx, sxln = s_x[q], s_xln[q]
        bx, bxln = b_x[q], b_xln[q]
        for k in range(NTB):
            dma(s_tst[0][:, :], d_x[t0 + 128 * k:t0 + 128 * (k + 1), :],
                [b_tst[0]], [b_tst[0]], sl_xin[0])
            for j in range(NDM):
                ps, bp = (ps_a, b_psa) if j < 4 else (ps_b, b_psb)
                p.add("tensor", (lambda j=j, k=k, ps=ps: (lambda e: e.transpose(
                    ps[:, 128 * (j % 4):128 * (j % 4 + 1)],
                    s_tst[0][:, 128 * j:128 * (j + 1)],
                    s_ident[:, :])))(),
                    reads=[b_tst[0], b_ident], writes=[bp])
            for half in range(2):
                ps, bp = (ps_a, b_psa) if half == 0 else (ps_b, b_psb)
                dst = sx[:, :].rearrange("p (j m) -> p j m", j=NDM)[
                    :, 4 * half:4 * half + 4, 128 * k:128 * (k + 1)]
                p.add("scalar", (lambda ps=ps, dst=dst: (lambda e: e.copy(
                    dst, ps[:, 0:512].rearrange("p (j m) -> p j m", j=4))))(),
                    reads=[bp], writes=[bx[4 * half + jj] for jj in range(4)])
            yield
        yield from layernorm(q)
        ps_xp, bp_xp = rot()
        for k in range(NDM):
            p.add("tensor", (lambda k=k, ps=ps_xp, sxln=sxln: (lambda e: e.matmul(
                ps[:, 0:TC], s_wa[:, 128 * k:128 * (k + 1)], sxln[:, TC * k:TC * (k + 1)],
                start=(k == 0), stop=(k == NDM - 1))))(),
                reads=[b_wa, bxln[k]], writes=[bp_xp])
        ha, bha = s_ha[q], b_ha[q]
        if c == 0:
            p.add("vector", (lambda ha=ha, ps=ps_xp: (lambda e: e.tensor_tensor_scan(
                ha[:, :], s_Aabc[:, :], ps[:, 0:TC], 0.0, ALU.mult, ALU.add)))(),
                reads=[b_Aabc, bp_xp], writes=[bha])
        else:
            hprev = s_ha[1 - q]
            p.add("vector", (lambda ha=ha, ps=ps_xp, hp=hprev: (lambda e: e.tensor_tensor_scan(
                ha[:, :], s_Aabc[:, :], ps[:, 0:TC], hp[:, TC - 1:TC], ALU.mult, ALU.add)))(),
                reads=[b_Aabc, bp_xp, b_ha[1 - q]], writes=[bha])
        p.add("scalar", (lambda ha=ha: (lambda e: e.copy(s_habf[:, :], ha[:, :])))(),
              reads=[bha], writes=[b_habf])
        yield
        for j in range(NDM):
            ps, bp = rot()
            p.add("tensor", (lambda j=j, ps=ps: (lambda e: e.matmul(
                ps[:, 0:TC], s_ca[:, 128 * j:128 * (j + 1)], s_habf[:, :],
                start=True, stop=True)))(),
                reads=[b_ca, b_habf], writes=[bp])
            p.add("vector", (lambda j=j, ps=ps, sx=sx: (lambda e: e.tensor_add(
                sx[:, TC * j:TC * (j + 1)], sx[:, TC * j:TC * (j + 1)], ps[:, 0:TC])))(),
                reads=[bx[j], bp], writes=[bx[j]])
            p.add("vector", (lambda j=j, sx=sx, sxln=sxln: (lambda e: e.scalar_tensor_tensor(
                sx[:, TC * j:TC * (j + 1)], sxln[:, TC * j:TC * (j + 1)],
                s_Da[:, j:j + 1], sx[:, TC * j:TC * (j + 1)], ALU.mult, ALU.add)))(),
                reads=[bxln[j], b_Da, bx[j]], writes=[bx[j]])
            if j % 2 == 1:
                yield
        yield from layernorm(q)
        yield from ffn(q)
        yield from layernorm(q)

    def heavy(c):
        q = c % 2
        sxln, bxln = s_xln[q], b_xln[q]
        pending_xd = []
        for jp in range(NDI // 2):
            pair = (2 * jp, 2 * jp + 1)
            for j in pair:
                dma(s_dg[j % 2][:, :], d_diag[j, :, :],
                    [b_dg[j % 2], b_diag], [b_dg[j % 2]], sl_dg)

            def c_hin(j):
                xb = s_xbs[j % 2]
                p.add("vector", (lambda j=j, xb=xb: (lambda e: e.tensor_copy(
                    xb[:, 0:3], s_halo[:, 3 * j:3 * (j + 1)])))(),
                    reads=[b_halo[j]], writes=[b_xbs[j % 2]])

            def c_hout(j):
                xb = s_xbs[j % 2]
                p.add("vector", (lambda j=j, xb=xb: (lambda e: e.tensor_copy(
                    s_halo[:, 3 * j:3 * (j + 1)], xb[:, TC:TC + 3])))(),
                    reads=[b_xbs[j % 2]], writes=[b_halo[j]])

            for j in pair:
                w = j % 2
                dma(s_wfp[w][:, :].rearrange("p (k m) -> p k m", k=NDM),
                    d_wint[:, :, j, :].transpose([1, 0, 2]),
                    [b_wfp[w], b_wint[j]], [b_wfp[w]], sl_wfp[w])
                c_hin(j)
                ps, bp = rot()
                for k in range(NDM):
                    p.add("tensor", (lambda k=k, ps=ps, w=w, sxln=sxln: (lambda e: e.matmul(
                        ps[:, 0:TC], s_wfp[w][:, 128 * k:128 * (k + 1)],
                        sxln[:, TC * k:TC * (k + 1)],
                        start=(k == 0), stop=(k == NDM - 1))))(),
                        reads=[b_wfp[w], bxln[k]], writes=[bp])
                xb = s_xbs[j % 2]
                p.add("scalar", (lambda ps=ps, xb=xb: (lambda e: e.copy(
                    xb[:, 3:TC + 3], ps[:, 0:TC])))(),
                    reads=[bp], writes=[b_xbs[j % 2]])
                yield

            def c_conv(j):
                xb = s_xbs[j % 2]
                ca = s_cacc if j % 2 == 0 else s_cacc2
                bca = b_cacc if j % 2 == 0 else b_cacc2
                ps, bp = rot()
                for k in range(4):
                    p.add("tensor", (lambda j=j, k=k, ps=ps, xb=xb: (lambda e: e.matmul(
                        ps[:, 0:TC], s_dg[j % 2][:, 128 * k:128 * (k + 1)],
                        xb[:, k:TC + k],
                        start=(k == 0), stop=(k == 3))))(),
                        reads=[b_dg[j % 2], b_xbs[j % 2]], writes=[bp])
                p.add("scalar", (lambda j=j, ps=ps, ca=ca: (lambda e: e.activation(
                    ca[0][:, :], ps[:, 0:TC], AF.Tanh, bias=s_cbh[:, j:j + 1],
                    scale=s_konst[:, 2:3])))(),
                    reads=[bp, b_cbh, b_konst], writes=[bca[0]])
                p.add("scalar", (lambda j=j, ps=ps, ca=ca: (lambda e: e.activation(
                    ca[1][:, :], ps[:, 0:TC], AF.Identity, scale=s_konst[:, 2:3],
                    bias=s_cbh[:, j:j + 1])))(),
                    reads=[bp, b_cbh, b_konst], writes=[bca[1]])

            def c_xc(j):
                sxc = s_xc[0]
                ca = s_cacc if j % 2 == 0 else s_cacc2
                bca = b_cacc if j % 2 == 0 else b_cacc2
                p.add("vector", (lambda j=j, sxc=sxc, ca=ca: (lambda e: e.scalar_tensor_tensor(
                    sxc[:, TC * j:TC * (j + 1)], ca[0][:, :], 1.0, ca[1][:, :],
                    ALU.add, ALU.mult)))(),
                    reads=[bca[0], bca[1]], writes=[b_xc[0][j]])
            j0, j1 = pair
            c_conv(j0)
            yield
            c_xc(j0); c_hout(j0)
            c_conv(j1)
            yield
            c_xc(j1); c_hout(j1)
            if pending_xd:
                pending_xd.pop(0)()

            def em_xd(pair=pair):
                for j in pair:
                    p.add("tensor", (lambda k=j, q=q: (lambda e: e.matmul(
                        ps_xd[0:96, 0:TC], s_wx[:, 96 * k:96 * (k + 1)],
                        s_xc[0][:, TC * k:TC * (k + 1)],
                        start=(k == 0), stop=(k == NDI - 1))))(),
                        reads=[b_wx, b_xc[0][j]], writes=[b_psxd])
            pending_xd.append(em_xd)
            yield "p"
        pending_xd.pop(0)()
        p.add("scalar", (lambda q=q: (lambda e: e.copy(s_dlow[0][:, :], ps_xd[0:DTR, 0:TC])))(),
              reads=[b_psxd], writes=[b_dlow[0]])
        p.add("scalar", (lambda q=q: (lambda e: e.copy(s_bc[0][:, :], ps_xd[DTR:96, 0:TC])))(),
              reads=[b_psxd], writes=[b_bc[0]])
        dma(d_bcr[0], s_bc[0][:, :], [b_bc[0], b_bcr[0]], [b_bcr[0]], sl_bcr[0])

    def scan(c, bg, hbg, bgn):
        q = c % 2
        sxln, bxln = s_xln[q], b_xln[q]
        sxc = s_xc[0]
        syg, byg = s_yg[0], b_yg[0]

        bg_done = [False]

        def pull(k):
            for _ in range(k):
                if next(bg, StopIteration) is StopIteration:
                    bg_done[0] = True
                    return

        hp = [0]

        def pull_h(k, allowed):
            if not bg_done[0]:
                pull(k)
                return
            for _ in range(k):
                if hp[0] >= allowed:
                    return
                v = next(hbg, StopIteration)
                if v is StopIteration:
                    return
                if v == "p":
                    hp[0] += 1

        pending_v = []
        zok = [False]
        zbanks = [(ps_m, b_psm), (ps_q, b_psq), (ps_rbc, b_psrbc),
                  (ps_mrbc, b_psmrbc)]
        for jb in range(NDI // JB):
            js = list(range(JB * jb, JB * (jb + 1)))

            def zpre(jj, js=tuple(js)):
                j = js[jj]
                w = j % 2
                dma(s_win[w][:, 0:NDM * 128].rearrange("p (k m) -> p k m", k=NDM),
                    d_wint[:, :, NDI + j, :].transpose([1, 0, 2]),
                    [b_win[w], b_wint[NDI + j]], [b_win[w]], sl_win[w])
                ps, bp = zbanks[jj]
                for k in range(NDM):
                    p.add("tensor", (lambda k=k, ps=ps, w=w, sxln=sxln: (lambda e: e.matmul(
                        ps[:, 0:TC], s_win[w][:, 128 * k:128 * (k + 1)],
                        sxln[:, TC * k:TC * (k + 1)],
                        start=(k == 0), stop=(k == NDM - 1))))(),
                        reads=[b_win[w], bxln[k]], writes=[bp])

            dtb = [(ps_dt, b_psdt), (ps_m, b_psm)] if jb >= 1 else                   [(ps_dt, b_psdt), (ps_dt, b_psdt)]

            def dt_mm(jj):
                j = js[jj]
                ps, bp = dtb[jj % 2]
                p.add("tensor", (lambda j=j, q=q, ps=ps: (lambda e: e.matmul(
                    ps[:, 0:TC], s_wdt[:, 128 * j:128 * (j + 1)], s_dlow[0][:, :],
                    start=True, stop=True)))(),
                    reads=[b_wdt, b_dlow[0]], writes=[bp])

            def dt_tanh(jj):
                j = js[jj]
                ps, bp = dtb[jj % 2]
                p.add("scalar", (lambda j=j, jj=jj, ps=ps: (lambda e: e.activation(
                    s_dt[jj][:, :], ps[:, 0:TC], AF.Tanh, bias=s_bdth[:, j:j + 1],
                    scale=s_konst[:, 2:3])))(),
                    reads=[bp, b_bdth, b_konst], writes=[b_dt[jj]])

            def dt_ts(jj):
                p.add("scalar", (lambda jj=jj: (lambda e: e.activation(
                    s_dt[jj][:, :], s_dt[jj][:, :], AF.Identity,
                    scale=s_konst[:, 3:4], bias=s_konst[:, 4:5])))(),
                    reads=[b_dt[jj], b_konst], writes=[b_dt[jj]])

            def dt_v(jj):
                j = js[jj]
                p.add("vector", (lambda j=j, jj=jj, sxc=sxc: (lambda e: e.tensor_mul(
                    s_v[jj][:, :], s_dt[jj][:, :], sxc[:, TC * j:TC * (j + 1)])))(),
                    reads=[b_dt[jj], b_xc[0][j]], writes=[b_v[jj]])
            dt_mm(0); dt_tanh(0); dt_mm(1); dt_ts(0); dt_tanh(1); dt_mm(2)
            dt_v(0); dt_ts(1); dt_tanh(2); dt_mm(3); dt_v(1); dt_ts(2)
            dt_tanh(3); dt_v(2); dt_ts(3); dt_v(3)
            for n in range(NST):
                np2 = n % 2
                dma(s_Bbc[np2][:, :],
                    bass.AP(d_bcr, TC * n, [[0, 128], [1, TC]]),
                    [b_bcr[0], b_Bbc[np2]], [b_Bbc[np2]], sl_bbc[np2])
                dma(s_Cbc[np2][:, :],
                    bass.AP(d_bcr, TC * (NST + n), [[0, 128], [1, TC]]),
                    [b_bcr[0], b_Cbc[np2]], [b_Cbc[np2]], sl_cbc[np2])

                def em_dA(jj, nn=None):
                    idx = NST * js[jj] + (n if nn is None else nn)
                    p.add("scalar", (lambda jj=jj, idx=idx: (lambda e: e.activation(
                        s_da[jj % 2][:, :], s_dt[jj][:, :], AF.Exp,
                        scale=s_Aneg[:, idx:idx + 1])))(),
                        reads=[b_dt[jj], b_Aneg], writes=[b_da[jj % 2]])

                def em_u(jj):
                    p.add("vector", (lambda jj=jj, np2=np2: (lambda e: e.tensor_mul(
                        s_u[jj % 2][:, :], s_v[jj][:, :], s_Bbc[np2][:, :])))(),
                        reads=[b_v[jj], b_Bbc[np2]], writes=[b_u[jj % 2]])

                def em_tts(jj):
                    h2 = jj % 2
                    idx = NST * js[jj] + n
                    if c == 0:
                        p.add("vector", (lambda jj=jj, h2=h2: (lambda e: e.tensor_tensor_scan(
                            s_h[h2][:, :], s_da[jj % 2][:, :], s_u[jj % 2][:, :], 0.0,
                            ALU.mult, ALU.add)))(),
                            reads=[b_da[jj % 2], b_u[jj % 2]], writes=[b_h[h2]])
                    else:
                        p.add("vector", (lambda jj=jj, h2=h2, idx=idx: (lambda e: e.tensor_tensor_scan(
                            s_h[h2][:, :], s_da[jj % 2][:, :], s_u[jj % 2][:, :],
                            s_Hcar[:, idx:idx + 1], ALU.mult, ALU.add)))(),
                            reads=[b_da[jj % 2], b_u[jj % 2], b_Hcar], writes=[b_h[h2]])

                def em_z(jj):
                    h2 = jj % 2
                    if n == 0:
                        p.add("vector", (lambda jj=jj, h2=h2, np2=np2: (lambda e: e.tensor_mul(
                            s_y[jj][:, :], s_h[h2][:, :], s_Cbc[np2][:, :])))(),
                            reads=[b_h[h2], b_Cbc[np2]], writes=[b_y[jj]])
                    else:
                        p.add("vector", (lambda jj=jj, h2=h2, np2=np2: (lambda e: e.tensor_mul(
                            s_zt[h2][:, :], s_h[h2][:, :], s_Cbc[np2][:, :])))(),
                            reads=[b_h[h2], b_Cbc[np2]], writes=[b_zt[h2]])

                def em_y(jj):
                    if n == 0:
                        return
                    h2 = jj % 2
                    p.add("vector", (lambda jj=jj, h2=h2: (lambda e: e.tensor_add(
                        s_y[jj][:, :], s_y[jj][:, :], s_zt[h2][:, :])))(),
                        reads=[b_zt[h2], b_y[jj]], writes=[b_y[jj]])

                def em_car(jj):
                    h2 = jj % 2
                    idx = NST * js[jj] + n
                    p.add("scalar", (lambda h2=h2, idx=idx: (lambda e: e.copy(
                        s_Hcar[:, idx:idx + 1], s_h[h2][:, TC - 1:TC])))(),
                        reads=[b_h[h2]], writes=[b_Hcar])
                if n == 0:
                    em_dA(0); em_dA(1)
                em_u(0); em_u(1); em_tts(0); em_dA(2); em_u(2); em_tts(1)
                pull(1)
                em_z(0); em_car(0); em_dA(3); em_u(3); em_tts(2)
                pull(1)
                em_z(1); em_car(1); em_y(0); em_tts(3)
                if pending_v:
                    pending_v.pop(0)()
                em_z(2); em_car(2); em_y(1)
                pull(1)
                em_z(3); em_car(3); em_y(2); em_y(3)
                if n + 1 < NST:
                    em_dA(0, n + 1); em_dA(1, n + 1)
                pull_h(2, 2 * jb)
                if n == NST - 4:
                    zok[0] = jb >= 1 or bg_done[0]
                if zok[0] and n >= NST - 4:
                    zpre(n - (NST - 4))
            for jj, j in enumerate(js):
                p.add("vector", (lambda j=j, jj=jj, sxc=sxc: (lambda e: e.scalar_tensor_tensor(
                    s_y[jj][:, :], sxc[:, TC * j:TC * (j + 1)], s_Db[:, j:j + 1],
                    s_y[jj][:, :], ALU.mult, ALU.add)))(),
                    reads=[b_xc[0][j], b_Db, b_y[jj]], writes=[b_y[jj]])
            for jj, j in enumerate(js):
                if zok[0]:
                    ps, bp = zbanks[jj]
                else:
                    w = j % 2
                    dma(s_win[w][:, 0:NDM * 128].rearrange("p (k m) -> p k m", k=NDM),
                        d_wint[:, :, NDI + j, :].transpose([1, 0, 2]),
                        [b_win[w], b_wint[NDI + j]], [b_win[w]], sl_win[w])
                    ps, bp = rot()
                    for k in range(NDM):
                        p.add("tensor", (lambda k=k, ps=ps, w=w, sxln=sxln: (lambda e: e.matmul(
                            ps[:, 0:TC], s_win[w][:, 128 * k:128 * (k + 1)],
                            sxln[:, TC * k:TC * (k + 1)],
                            start=(k == 0), stop=(k == NDM - 1))))(),
                            reads=[b_win[w], bxln[k]], writes=[bp])
                p.add("scalar", (lambda ps=ps, bp=bp, u2=jj % 2: (lambda e: e.activation(
                    s_zt[u2][:, :], ps[:, 0:TC], AF.Tanh, scale=s_konst[:, 2:3])))(),
                    reads=[bp, b_konst], writes=[b_zt[jj % 2]])
                p.add("vector", (lambda ps=ps, bp=bp, u2=jj % 2, jj=jj: (lambda e: e.scalar_tensor_tensor(
                    s_da[jj % 2][:, :], s_zt[u2][:, :], 1.0, ps[:, 0:TC], ALU.add, ALU.mult)))(),
                    reads=[b_zt[jj % 2], bp], writes=[b_da[jj % 2]])
                p.add("vector", (lambda j=j, jj=jj, syg=syg: (lambda e: e.scalar_tensor_tensor(
                    syg[:, TC * j:TC * (j + 1)], s_da[jj % 2][:, :], 0.5, s_y[jj][:, :],
                    ALU.mult, ALU.mult)))(),
                    reads=[b_da[jj % 2], b_y[jj]], writes=[byg[j]])
                pull(2)

            def wo_group(jb=jb, js=tuple(js)):
                def emit(f):
                    w = f % 2
                    dma(s_wo2[w][:, :].rearrange("p (k m) -> p k m", k=JB),
                        d_woutt[JB * jb:JB * (jb + 1), :, f, :].transpose([1, 0, 2]),
                        [b_wo2[w], b_woutt[f]], [b_wo2[w]], sl_wo2[w])
                    ps, bp = rot()
                    for kk, k in enumerate(js):
                        p.add("tensor", (lambda k=k, ps=ps, w=w, kk=kk, syg=syg: (lambda e: e.matmul(
                            ps[:, 0:TC], s_wo2[w][:, 128 * kk:128 * (kk + 1)],
                            syg[:, TC * k:TC * (k + 1)],
                            start=(kk == 0), stop=(kk == JB - 1))))(),
                            reads=[b_wo2[w], byg[k]], writes=[bp])
                    p.add("vector", (lambda f=f, ps=ps, q=q: (lambda e: e.tensor_add(
                        s_x[q][:, TC * f:TC * (f + 1)], s_x[q][:, TC * f:TC * (f + 1)],
                        ps[:, 0:TC])))(),
                        reads=[b_x[q][f], bp], writes=[b_x[q][f]])
                return [(lambda f=f: emit(f)) for f in range(NDM)]
            pending_v.extend(wo_group())
        while next(hbg, StopIteration) is not StopIteration:
            pass

        def bgn_pull(k):
            for _ in range(k):
                if next(bgn, StopIteration) is StopIteration:
                    return
        for i, fn in enumerate(pending_v):
            fn()
            if i == 3 or i == 7:
                bgn_pull(1)
        bgn_pull(9)

    def post(c):
        q = c % 2
        t0 = c * TC
        sx, bx = s_x[q], b_x[q]
        yield from layernorm(q)
        yield from ffn(q)
        for k in range(NTB):
            for j in range(NDM):
                ps, bp = (ps_a, b_psa) if j < 4 else (ps_b, b_psb)
                p.add("tensor", (lambda j=j, k=k, ps=ps, sx=sx: (lambda e: e.transpose(
                    ps[:, 128 * (j % 4):128 * (j % 4 + 1)],
                    sx[:, TC * j + 128 * k:TC * j + 128 * (k + 1)], s_ident[:, :])))(),
                    reads=[bx[j], b_ident], writes=[bp])
            p.add("scalar", (lambda k=k: (lambda e: e.copy(
                s_tst[0][:, 0:512], ps_a[:, 0:512])))(),
                reads=[b_psa], writes=[b_tst[0]])
            p.add("scalar", (lambda k=k: (lambda e: e.copy(
                s_tst[0][:, 512:DM], ps_b[:, 0:512])))(),
                reads=[b_psb], writes=[b_tst[0]])
            dma(d_y[t0 + 128 * k:t0 + 128 * (k + 1), :], s_tst[0][:, :],
                [b_tst[0]], [b_tst[0]], sl_xout[0])
            yield

    def empty_gen():
        return
        yield

    def chain2(g1, g2):
        for v in g1:
            yield v
        for v in g2:
            yield v

    def mk_bg(c):
        return chain2(post(c - 1) if c > 0 else empty_gen(),
                      front(c + 1) if c + 1 < NCH else empty_gen())

    iw = init_wio()
    fg = front(0)
    while True:
        a = next(iw, StopIteration)
        b = next(fg, StopIteration)
        if a is StopIteration and b is StopIteration:
            break
    for _ in heavy(0):
        pass
    bg = mk_bg(0)
    for c in range(NCH):
        bgn = mk_bg(c + 1) if c + 1 < NCH else empty_gen()
        hbg = heavy(c + 1) if c + 1 < NCH else empty_gen()
        scan(c, bg, hbg, bgn)
        for _ in bg:
            pass
        for _ in hbg:
            pass
        bg = bgn
    for _ in bg:
        pass
    for _ in post(NCH - 1):
        pass

    p.finalize(nc, stack)
    with stack:
        with nc.Block() as block:
            @block.sync
            def _(sync):
                p.emit_engine("sync", sync)
                sync.wait_ge(sl_xout[0].sem, sl_xout[0].count)

            @block.tensor
            def _(t):
                p.emit_engine("tensor", t)

            @block.scalar
            def _(sc):
                p.emit_engine("scalar", sc)

            @block.vector
            def _(v):
                p.emit_engine("vector", v)
    return nc


def _host_inputs(inputs, b, L):
    f32 = lambda a: np.ascontiguousarray(np.asarray(a, dtype=np.float32))
    x = f32(inputs["L0_attn_a_x"])[b, :L]
    return {
        "x": x,
        "WaT": f32(np.asarray(inputs["L0_attn_a_B_w"]).T),
        "CaT": f32(np.asarray(inputs["L0_attn_a_C_w"]).T),
        "la": f32(np.asarray(inputs["L0_attn_a_log_A"]).reshape(128, 1)),
        "Da": f32(np.asarray(inputs["L0_attn_a_D_param"]).reshape(NDM, 128).T),
        "Win": f32(inputs["L0_attn_b_W_in"]),
        "cw": f32(np.asarray(inputs["L0_attn_b_conv_w"]).reshape(DI, 4)
                  .reshape(NDI, 128, 4).transpose(1, 0, 2).reshape(128, 4 * NDI)),
        "cb": f32(np.asarray(inputs["L0_attn_b_conv_b"]).reshape(NDI, 128).T),
        "Wx": f32(inputs["L0_attn_b_W_x"]),
        "Wdt": f32(inputs["L0_attn_b_W_dt"]),
        "bdt": f32(np.asarray(inputs["L0_attn_b_b_dt"]).reshape(NDI, 128).T),
        "Alog": f32(np.asarray(inputs["L0_attn_b_A_log"]).reshape(NDI, 128, NST)
                    .transpose(1, 0, 2).reshape(128, NDI * NST)),
        "Db": f32(np.asarray(inputs["L0_attn_b_D_param"]).reshape(NDI, 128).T),
        "Wout": f32(inputs["L0_attn_b_W_out"]),
        "WfT": f32(np.asarray(inputs["L0_ffn_weight"]).T),
        "ident": np.eye(128, dtype=np.float32),
        "onesred": np.stack([np.full(128, 1.0 / DM, np.float32),
                             np.ones(128, np.float32)], 1),
        "onesrow": np.ones((1, 128), np.float32),
    }


def kernel(**inputs):
    xfull = np.asarray(inputs["L0_attn_a_x"])
    nb, L = int(xfull.shape[0]), int(xfull.shape[1])
    nc = build_nc(L)
    in_maps = [_host_inputs(inputs, b, L) for b in range(nb)]
    res = run_bass_kernel_spmd(nc, in_maps, core_ids=list(range(nb)))
    out = np.stack([np.asarray(r["y"]) for r in res.results], axis=0)
    return out.astype(np.float32)


# version 31
# speedup vs baseline: 1.0073x; 1.0073x over previous
import numpy as np
import concourse.bass as bass
import concourse.mybir as mybir
from concourse.bass_utils import run_bass_kernel_spmd
from contextlib import ExitStack

F32 = mybir.dt.float32
BF16 = mybir.dt.bfloat16
AF = mybir.ActivationFunctionType
ALU = mybir.AluOpType
AX = mybir.AxisListType

DM = 1024
DI = 2048
NST = 16
DTR = 64
TC = 512
NTB = TC // 128
NDM = DM // 128
NDI = DI // 128
JB = 4
LN_EPS = 1e-5


class _Buf:
    __slots__ = ("writer", "readers")

    def __init__(self):
        self.writer = None
        self.readers = []


class _Op:
    __slots__ = ("eng", "emit", "deps", "eng_pos", "sem_val", "slot", "needs_inc")


class _Slot:
    __slots__ = ("name", "count", "sem")

    def __init__(self, name):
        self.name = name
        self.count = 0
        self.sem = None


class Plan:
    ENGINES = ("sync", "tensor", "scalar", "vector")

    def __init__(self):
        self.eng_ops = {e: [] for e in self.ENGINES}
        self.all_ops = []
        self.slots = []

    def slot(self, name):
        s = _Slot(name)
        self.slots.append(s)
        return s

    def add(self, eng, emit, reads=(), writes=(), slot=None):
        op = _Op()
        op.eng = eng
        op.emit = emit
        op.slot = slot
        op.needs_inc = False
        op.sem_val = None
        deps = set()
        for b in reads:
            if b.writer is not None:
                deps.add(b.writer)
        for b in writes:
            for r in b.readers:
                deps.add(r)
            if b.writer is not None:
                deps.add(b.writer)
        deps.discard(op)
        op.deps = deps
        op.eng_pos = len(self.eng_ops[eng])
        self.eng_ops[eng].append(op)
        self.all_ops.append(op)
        if slot is not None:
            op.sem_val = slot.count + 16
            slot.count += 16
        for b in reads:
            b.readers.append(op)
        for b in writes:
            b.writer = op
            b.readers = []
        return op

    def finalize(self, nc, stack):
        for op in self.all_ops:
            for d in op.deps:
                if d.slot is None:
                    d.needs_inc = True
        eng_sem = {e: stack.enter_context(nc.semaphore("sem_" + e))
                   for e in self.ENGINES}
        for s in self.slots:
            s.sem = stack.enter_context(nc.semaphore("dma_" + s.name))
        counts = {e: 0 for e in self.ENGINES}
        for e in self.ENGINES:
            for op in self.eng_ops[e]:
                if op.slot is None and op.needs_inc:
                    counts[e] += 1
                    op.sem_val = counts[e]
        self.eng_sem = eng_sem

    def emit_engine(self, eng_name, eng_obj):
        watermark = {}
        for op in self.eng_ops[eng_name]:
            waits = {}
            if op.slot is not None and op.sem_val > 16:
                waits[id(op.slot.sem)] = (op.slot.sem, op.sem_val - 16)
            for d in op.deps:
                sem = d.slot.sem if d.slot is not None else self.eng_sem[d.eng]
                k = id(sem)
                if k not in waits or waits[k][1] < d.sem_val:
                    waits[k] = (sem, d.sem_val)
            for sem, val in waits.values():
                k = id(sem)
                if watermark.get(k, 0) < val:
                    eng_obj.wait_ge(sem, val)
                    watermark[k] = val
            ins = op.emit(eng_obj)
            if op.slot is not None:
                ins.then_inc(op.slot.sem, 16)
            elif op.needs_inc:
                ins.then_inc(self.eng_sem[op.eng], 1)


def build_nc(L):
    NCH = L // TC
    nc = bass.Bass("TRN2", target_bir_lowering=False)
    p = Plan()

    d_x = nc.dram_tensor("x", [L, DM], F32, kind="ExternalInput")
    d_WaT = nc.dram_tensor("WaT", [DM, 128], F32, kind="ExternalInput")
    d_CaT = nc.dram_tensor("CaT", [128, DM], F32, kind="ExternalInput")
    d_la = nc.dram_tensor("la", [128, 1], F32, kind="ExternalInput")
    d_Da = nc.dram_tensor("Da", [128, NDM], F32, kind="ExternalInput")
    d_Win = nc.dram_tensor("Win", [DM, 2 * DI], F32, kind="ExternalInput")
    d_cw = nc.dram_tensor("cw", [128, 4 * NDI], F32, kind="ExternalInput")
    d_cb = nc.dram_tensor("cb", [128, NDI], F32, kind="ExternalInput")
    d_Wx = nc.dram_tensor("Wx", [DI, 96], F32, kind="ExternalInput")
    d_Wdt = nc.dram_tensor("Wdt", [DTR, DI], F32, kind="ExternalInput")
    d_bdt = nc.dram_tensor("bdt", [128, NDI], F32, kind="ExternalInput")
    d_Alog = nc.dram_tensor("Alog", [128, NDI * NST], F32, kind="ExternalInput")
    d_Db = nc.dram_tensor("Db", [128, NDI], F32, kind="ExternalInput")
    d_Wout = nc.dram_tensor("Wout", [DI, DM], F32, kind="ExternalInput")
    d_WfT = nc.dram_tensor("WfT", [DM, DM], F32, kind="ExternalInput")
    d_ident = nc.dram_tensor("ident", [128, 128], F32, kind="ExternalInput")
    d_onesred = nc.dram_tensor("onesred", [128, 2], F32, kind="ExternalInput")
    d_onesrow = nc.dram_tensor("onesrow", [1, 128], F32, kind="ExternalInput")
    d_y = nc.dram_tensor("y", [L, DM], F32, kind="ExternalOutput")
    d_wint = nc.dram_tensor("wint", [NDM, 128, 2 * NDI, 128], BF16)
    d_woutt = nc.dram_tensor("woutt", [NDI, 128, NDM, 128], BF16)
    d_wft = nc.dram_tensor("wft", [NDM, 128, NDM, 128], BF16)
    d_diag = nc.dram_tensor("cdiag", [NDI, 128, 4 * 128], BF16)
    d_bcr = nc.dram_tensor("bcr", [1, 32, TC], F32)
    d_srow = nc.dram_tensor("srow", [1, 4], F32)

    stack = ExitStack()
    sb = lambda name, shape, dt: stack.enter_context(nc.sbuf_tensor(name, shape, dt))
    psum = lambda name: stack.enter_context(nc.psum_tensor(name, [128, 512], F32))

    s_wa = sb("s_wa", [128, NDM * 128], BF16)
    s_ca = sb("s_ca", [128, DM], BF16)
    s_wx = sb("s_wx", [128, NDI * 96], F32)
    s_wdt = sb("s_wdt", [DTR, DI], BF16)
    s_win = [sb("s_win%d" % i, [128, NDI * 128], BF16) for i in range(2)]
    s_wo2 = [sb("s_wo2%d" % i, [128, JB * 128], BF16) for i in range(2)]
    s_wfp = [sb("s_wfp%d" % i, [128, NDM * 128], BF16) for i in range(2)]
    s_Aneg = sb("s_Aneg", [128, NDI * NST], F32)
    s_Aa = sb("s_Aa", [128, 1], F32)
    s_Aabc = sb("s_Aabc", [128, TC], F32)
    s_Da = sb("s_Da", [128, NDM], F32)
    s_Db = sb("s_Db", [128, NDI], F32)
    s_cw = sb("s_cw", [128, 4 * NDI], F32)
    s_cb = sb("s_cb", [128, NDI], F32)
    s_cbh = sb("s_cbh", [128, NDI], F32)
    s_bdth = sb("s_bdth", [128, NDI], F32)
    s_ident = sb("s_ident", [128, 128], F32)
    s_onesred = sb("s_onesred", [128, 2], F32)
    s_onesrow = sb("s_onesrow", [1, 128], F32)
    s_thr = sb("s_thr", [128, 3], F32)
    s_konst = sb("s_konst", [128, 5], F32)

    s_tst = [sb("s_tst0", [128, DM], F32)]
    s_x = [sb("s_x%d" % q, [128, NDM * TC], F32) for q in range(2)]
    s_xln = [sb("s_xln%d" % q, [128, NDM * TC], BF16) for q in range(2)]
    s_mrow = sb("s_mrow", [1, TC], F32)
    s_varrow = sb("s_varrow", [1, TC], F32)
    s_xbs = [sb("s_xbs%d" % i, [128, TC + 3], BF16) for i in range(2)]
    s_cacc = [sb("s_cacc%d" % i, [128, TC], F32) for i in range(2)]
    s_halo = sb("s_halo", [128, NDI * 3], BF16)
    s_dg = [sb("s_dg%d" % i, [128, 4 * 128], BF16) for i in range(2)]
    s_cacc2 = [sb("s_cacc2%d" % i, [128, TC], F32) for i in range(2)]
    s_xc = [sb("s_xc0", [128, NDI * TC], F32)]
    s_dlow = [sb("s_dlow0", [DTR, TC], BF16)]
    s_bc = [sb("s_bc0", [32, TC], F32)]
    s_Bbc = [sb("s_Bbc%d" % i, [128, TC], F32) for i in range(2)]
    s_Cbc = [sb("s_Cbc%d" % i, [128, TC], F32) for i in range(2)]
    s_dt = [sb("s_dt%d" % i, [128, TC], F32) for i in range(JB)]
    s_v = [sb("s_v%d" % i, [128, TC], F32) for i in range(JB)]
    s_da = [sb("s_da%d" % i, [128, TC], F32) for i in range(2)]
    s_u = [sb("s_u%d" % i, [128, TC], F32) for i in range(2)]
    s_h = [sb("s_h%d" % i, [128, TC], F32) for i in range(2)]
    s_zt = [sb("s_zt%d" % i, [128, TC], F32) for i in range(2)]
    s_Hcar = sb("s_Hcar", [128, NDI * NST], F32)
    s_y = [sb("s_y%d" % i, [128, TC], F32) for i in range(JB)]
    s_yg = [sb("s_yg0", [128, NDI * TC], BF16)]
    s_ha = [sb("s_ha%d" % q, [128, TC], F32) for q in range(2)]
    s_habf = sb("s_habf", [128, TC], BF16)
    s_small = sb("s_small", [128, 8], F32)
    s_srow = sb("s_srow", [1, 4], F32)

    ps_a, ps_b = psum("ps_a"), psum("ps_b")
    ps_xd, ps_dt = psum("ps_xd"), psum("ps_dt")
    ps_m, ps_q = psum("ps_m"), psum("ps_q")
    ps_rbc, ps_mrbc = psum("ps_rbc"), psum("ps_mrbc")

    B = _Buf
    b_wa, b_ca, b_wx, b_wdt = B(), B(), B(), B()
    b_wfp = [B(), B()]
    b_wft = [B() for _ in range(NDM)]
    b_win = [B(), B()]
    b_wo2 = [B(), B()]
    b_Aneg, b_Aa, b_Aabc, b_Da, b_Db = B(), B(), B(), B(), B()
    b_cw, b_cb, b_cbh, b_bdth = B(), B(), B(), B()
    b_ident, b_onesred, b_onesrow, b_thr, b_konst = B(), B(), B(), B(), B()
    b_tst = [B(), B()]
    b_x = [[B() for _ in range(NDM)] for _ in range(2)]
    b_xln = [[B() for _ in range(NDM)] for _ in range(2)]
    b_mrow, b_varrow, b_rrow = B(), B(), B()
    b_xbs = [B(), B()]
    b_dg = [B(), B()]
    b_diag = B()
    b_cacc2 = [B(), B()]
    b_cacc = [B(), B()]
    b_halo = [B() for _ in range(NDI)]
    b_xc = [[B() for _ in range(NDI)]]
    b_dlow = [B()]
    b_bc = [B()]
    b_bcr = [B()]
    b_Bbc = [B(), B()]
    b_Cbc = [B(), B()]
    b_dt = [B() for _ in range(JB)]
    b_v = [B() for _ in range(JB)]
    b_da = [B(), B()]
    b_u = [B(), B()]
    b_h = [B(), B()]
    b_zt = [B(), B()]
    b_Hcar = B()
    b_y = [B() for _ in range(JB)]
    b_yg = [[B() for _ in range(NDI)]]
    b_ha = [B(), B()]
    b_habf = B()
    b_small, b_srow = B(), B()
    b_psa, b_psb, b_psxd, b_psdt = B(), B(), B(), B()
    b_psm, b_psq, b_psrbc, b_psmrbc = B(), B(), B(), B()
    b_wint = [B() for _ in range(2 * NDI)]
    b_woutt = [B() for _ in range(NDM)]

    sl_init = p.slot("init")
    sl_wstage = p.slot("wstage")
    sl_dg = p.slot("dg")
    sl_ws = [p.slot("ws%d" % i) for i in range(4)]
    sl_wt = [p.slot("wt%d" % i) for i in range(2)]
    sl_wtile = p.slot("wtile")
    sl_xin = [p.slot("xin%d" % k) for k in range(2)]
    sl_xout = [p.slot("xout%d" % k) for k in range(2)]
    sl_win = [p.slot("win%d" % i) for i in range(2)]
    sl_wo2 = [p.slot("wo2%d" % i) for i in range(2)]
    sl_wfp = [p.slot("wfp%d" % i) for i in range(2)]
    sl_bbc = [p.slot("bbc%d" % i) for i in range(2)]
    sl_cbc = [p.slot("cbc%d" % i) for i in range(2)]
    sl_bcr = [p.slot("bcr0")]
    sl_thr = p.slot("thr")

    _rot = [(ps_a, b_psa), (ps_b, b_psb)]
    _rot_i = [0]

    def rot():
        t = _rot[_rot_i[0] % 2]
        _rot_i[0] += 1
        return t

    def dma(dst_ap, src_ap, reads, writes, slot):
        return p.add("sync",
                     (lambda d=dst_ap, s=src_ap: (lambda e: e.dma_start(out=d, in_=s)))(),
                     reads=reads, writes=writes, slot=slot)

    dma(s_ident[:, :], d_ident[:, :], [], [b_ident], sl_init)
    dma(s_onesred[:, :], d_onesred[:, :], [], [b_onesred], sl_init)
    dma(s_onesrow[:, :], d_onesrow[:, :], [], [b_onesrow], sl_init)
    dma(s_Da[:, :], d_Da[:, :], [], [b_Da], sl_init)
    dma(s_Db[:, :], d_Db[:, :], [], [b_Db], sl_init)
    dma(s_cw[:, :], d_cw[:, :], [], [b_cw], sl_init)
    dma(s_cb[:, :], d_cb[:, :], [], [b_cb], sl_init)
    dma(s_cbh[:, :], d_cb[:, :], [], [b_cbh], sl_init)
    dma(s_bdth[:, :], d_bdt[:, :], [], [b_bdth], sl_init)
    dma(s_wx[:, :].rearrange("p (k m) -> p k m", k=NDI),
        d_Wx.rearrange("(k p) m -> p k m", p=128), [], [b_wx], sl_init)
    dma(s_Aneg[:, :], d_Alog[:, :], [], [b_Aneg], sl_init)
    dma(s_Aa[:, :], d_la[:, :], [], [b_Aa], sl_init)

    p.add("vector", lambda e: e.tensor_scalar_mul(s_cbh[:, :], s_cbh[:, :], 0.5),
          reads=[b_cbh], writes=[b_cbh])
    p.add("vector", lambda e: e.tensor_scalar_mul(s_bdth[:, :], s_bdth[:, :], 0.5),
          reads=[b_bdth], writes=[b_bdth])
    p.add("scalar", lambda e: e.activation(s_Aneg[:, :], s_Aneg[:, :], AF.Exp),
          reads=[b_Aneg], writes=[b_Aneg])
    p.add("vector", lambda e: e.tensor_scalar_mul(s_Aneg[:, :], s_Aneg[:, :], -1.0),
          reads=[b_Aneg], writes=[b_Aneg])
    p.add("scalar", lambda e: e.activation(s_Aa[:, :], s_Aa[:, :], AF.Exp),
          reads=[b_Aa], writes=[b_Aa])
    p.add("vector", lambda e: e.memset(s_Aabc[:, :], 1.0), reads=[], writes=[b_Aabc])
    p.add("vector", lambda e: e.tensor_scalar_mul(s_Aabc[:, :], s_Aabc[:, :], s_Aa[:, 0:1]),
          reads=[b_Aa, b_Aabc], writes=[b_Aabc])
    p.add("vector", lambda e: e.memset(s_halo[:, :], 0.0), reads=[], writes=b_halo)
    p.add("vector", lambda e: e.memset(s_konst[:, 0:1], LN_EPS), reads=[b_konst], writes=[b_konst])
    p.add("vector", lambda e: e.memset(s_konst[:, 1:2], -0.5), reads=[b_konst], writes=[b_konst])
    p.add("vector", lambda e: e.memset(s_konst[:, 2:3], 0.5), reads=[b_konst], writes=[b_konst])
    p.add("vector", lambda e: e.memset(s_konst[:, 3:4], 0.0495), reads=[b_konst], writes=[b_konst])
    p.add("vector", lambda e: e.memset(s_konst[:, 4:5], 0.0505), reads=[b_konst], writes=[b_konst])

    stage = s_xc[0]
    stage2 = s_yg[0]
    HALFC = NDI * TC // 2
    stg = lambda h, n: stage[:, HALFC * h:HALFC * h + n]
    stg2 = lambda h, n: stage2[:, HALFC * h:HALFC * h + n]
    bstg = [b_xc[0][0:8], b_xc[0][8:16]]
    bstg2 = [b_yg[0][0:8], b_yg[0][8:16]]
    QC = NDI * TC // 4
    stq = lambda h, n: stage[:, QC * h:QC * h + n]
    stq2 = lambda h, n: stage2[:, QC * h:QC * h + n]
    bstq = [b_xc[0][4 * i:4 * i + 4] for i in range(4)]
    bstq2 = [b_yg[0][4 * i:4 * i + 4] for i in range(4)]
    bstage = b_xc[0]
    bstage2 = b_yg[0]

    dma(stage[:, 0:NDM * 128].rearrange("p (k m) -> p k m", m=128),
        d_WaT.rearrange("(k p) m -> p k m", p=128), bstage, bstage, sl_wstage)
    p.add("vector", lambda e: e.tensor_copy(s_wa[:, :], stage[:, 0:NDM * 128]),
          reads=bstage, writes=[b_wa])
    dma(stage[:, 0:DM], d_CaT[:, :], bstage, bstage, sl_wstage)
    p.add("vector", lambda e: e.tensor_copy(s_ca[:, :], stage[:, 0:DM]),
          reads=bstage, writes=[b_ca])
    dma(stage[0:DTR, 0:DI], d_Wdt[:, :], bstage, bstage, sl_wstage)
    p.add("vector", lambda e: e.tensor_copy(s_wdt[:, :], stage[0:DTR, 0:DI]),
          reads=bstage, writes=[b_wdt])
    for k in range(NDM):
        h = k % 4
        dma(stq(h, DM), d_WfT[128 * k:128 * (k + 1), :], bstq[h], bstq[h], sl_ws[k % 4])
        p.add("vector", (lambda k=k, h=h: (lambda e: e.tensor_reduce(
            s_small[:, k:k + 1], stq(h, DM),
            AX.X, ALU.add, apply_absolute_value=True)))(),
            reads=bstq[h], writes=[b_small])
    p.add("tensor", lambda e: e.matmul(ps_a[0:1, 0:8], s_onesred[:, 1:2], s_small[:, :],
                                       start=True, stop=True),
          reads=[b_onesred, b_small], writes=[b_psa])
    p.add("vector", lambda e: e.reduce_sum(s_srow[0:1, 0:1], ps_a[0:1, 0:8], AX.X),
          reads=[b_psa], writes=[b_srow])
    p.add("vector", lambda e: e.tensor_scalar(s_srow[0:1, 0:1], s_srow[0:1, 0:1],
                                              1.0 / (DM * DM), 1e-5, ALU.mult, ALU.max),
          reads=[b_srow], writes=[b_srow])
    p.add("vector", lambda e: e.tensor_scalar_mul(s_srow[0:1, 1:2], s_srow[0:1, 0:1], 0.5),
          reads=[b_srow], writes=[b_srow])
    p.add("vector", lambda e: e.tensor_scalar_mul(s_srow[0:1, 2:3], s_srow[0:1, 0:1], -0.5),
          reads=[b_srow], writes=[b_srow])
    dma(d_srow[0:1, 0:3], s_srow[0:1, 0:3], [b_srow], [b_srow], sl_thr)
    dma(s_thr[:, :], bass.AP(d_srow, 0, [[0, 128], [1, 3]]), [b_srow], [b_thr], sl_thr)
    for k in range(NDM):
        h = k % 4
        dma(stq(h, DM), d_WfT[128 * k:128 * (k + 1), :], bstq[h], bstq[h], sl_ws[k % 4])
        for qq in range(2):
            lo = TC * qq
            p.add("vector", (lambda lo=lo, h=h: (lambda e: e.tensor_single_scalar(
                s_cacc[0][:, :], stq(h, DM)[:, lo:lo + TC], s_thr[:, 1:2], ALU.is_gt)))(),
                reads=bstq[h] + [b_thr], writes=[b_cacc[0]])
            p.add("vector", (lambda lo=lo, h=h: (lambda e: e.tensor_single_scalar(
                s_cacc[1][:, :], stq(h, DM)[:, lo:lo + TC], s_thr[:, 2:3], ALU.is_lt)))(),
                reads=bstq[h] + [b_thr], writes=[b_cacc[1]])
            p.add("vector", (lambda lo=lo, h=h: (lambda e: e.tensor_sub(
                stq2(h, DM)[:, lo:lo + TC], s_cacc[0][:, :], s_cacc[1][:, :])))(),
                reads=[b_cacc[0], b_cacc[1]], writes=bstq2[h])
        dma(d_wft[k], stq2(h, DM).rearrange("p (f m) -> p f m", f=NDM),
            bstq2[h], b_wft, sl_wt[k % 2])

    def init_wio():
        for k in range(NDM):
            h = k % 2
            dma(stg(h, 2 * DI), d_Win[128 * k:128 * (k + 1), :],
                bstg[h], bstg[h], sl_ws[k % 4])
            p.add("vector", (lambda h=h: (lambda e: e.tensor_copy(
                stg2(h, 2 * DI), stg(h, 2 * DI))))(),
                reads=bstg[h], writes=bstg2[h])
            dma(d_wint[k], stg2(h, 2 * DI).rearrange("p (j f) -> p j f", j=2 * NDI),
                bstg2[h], b_wint, sl_wt[k % 2])
            yield
        for k in range(NDI):
            h = k % 4
            dma(stq(h, DM), d_Wout[128 * k:128 * (k + 1), :],
                bstq[h], bstq[h], sl_ws[k % 4])
            p.add("vector", (lambda h=h: (lambda e: e.tensor_copy(
                stq2(h, DM), stq(h, DM))))(),
                reads=bstq[h], writes=bstq2[h])
            dma(d_woutt[k], stq2(h, DM).rearrange("p (f m) -> p f m", f=NDM),
                bstq2[h], b_woutt, sl_wt[k % 2])
            yield
        for j in range(NDI):
            h = j % 4
            for k in range(4):
                p.add("vector", (lambda j=j, k=k, h=h: (lambda e: e.tensor_single_scalar(
                    stq(h, TC)[:, 128 * k:128 * (k + 1)], s_ident[:, :],
                    s_cw[:, 4 * j + k:4 * j + k + 1], ALU.mult)))(),
                    reads=[b_ident, b_cw] + bstq[h], writes=bstq[h])
            p.add("vector", (lambda h=h: (lambda e: e.tensor_copy(
                stq2(h, TC), stq(h, TC))))(),
                reads=bstq[h], writes=bstq2[h])
            dma(d_diag[j], stq2(h, TC), bstq2[h], [b_diag], sl_wt[j % 2])
            yield

    def layernorm(q):
        sx, sxln = s_x[q], s_xln[q]
        bx, bxln = b_x[q], b_xln[q]
        for j in range(NDM):
            if j in (2, 4, 6):
                yield
            sq = s_cacc[j % 2]
            p.add("scalar", (lambda j=j, sq=sq, sx=sx: (lambda e: e.activation(
                sq[:, :], sx[:, TC * j:TC * (j + 1)], AF.Square)))(),
                reads=[bx[j]], writes=[b_cacc[j % 2]])
            p.add("tensor", (lambda j=j, sx=sx: (lambda e: e.matmul(
                ps_m[0:1, 0:TC], s_onesred[:, 0:1], sx[:, TC * j:TC * (j + 1)],
                start=(j == 0), stop=(j == NDM - 1), skip_group_check=True)))(),
                reads=[b_onesred, bx[j]], writes=[b_psm])
            p.add("tensor", (lambda j=j, sq=sq: (lambda e: e.matmul(
                ps_q[0:1, 0:TC], s_onesred[:, 0:1], sq[:, :],
                start=(j == 0), stop=(j == NDM - 1), skip_group_check=True)))(),
                reads=[b_onesred, b_cacc[j % 2]], writes=[b_psq])
        p.add("vector", lambda e: e.tensor_copy(s_mrow[0:1, :], ps_m[0:1, 0:TC]),
              reads=[b_psm], writes=[b_mrow])
        p.add("vector", lambda e: e.tensor_mul(s_varrow[0:1, :], s_mrow[0:1, :], s_mrow[0:1, :]),
              reads=[b_mrow], writes=[b_varrow])
        p.add("vector", lambda e: e.tensor_sub(s_varrow[0:1, :], ps_q[0:1, 0:TC], s_varrow[0:1, :]),
              reads=[b_psq, b_varrow], writes=[b_varrow])
        p.add("scalar", lambda e: e.activation(s_varrow[0:1, :], s_varrow[0:1, :], AF.Ln,
                                               bias=s_konst[0:1, 0:1]),
              reads=[b_varrow, b_konst], writes=[b_varrow])
        p.add("scalar", lambda e: e.activation(s_varrow[0:1, :], s_varrow[0:1, :], AF.Exp,
                                               scale=s_konst[0:1, 1:2]),
              reads=[b_varrow, b_konst], writes=[b_rrow])
        p.add("vector", lambda e: e.tensor_mul(s_mrow[0:1, :], s_mrow[0:1, :], s_varrow[0:1, :]),
              reads=[b_mrow, b_rrow], writes=[b_mrow])
        p.add("tensor", lambda e: e.matmul(ps_rbc[:, 0:TC], s_onesrow[:, :], s_varrow[0:1, :],
                                           start=True, stop=True),
              reads=[b_onesrow, b_rrow], writes=[b_psrbc])
        p.add("tensor", lambda e: e.matmul(ps_mrbc[:, 0:TC], s_onesrow[:, :], s_mrow[0:1, :],
                                           start=True, stop=True),
              reads=[b_onesrow, b_mrow], writes=[b_psmrbc])
        yield

        def ap_mul(j):
            sq = s_cacc[j % 2]
            p.add("vector", (lambda j=j, sq=sq, sx=sx: (lambda e: e.tensor_mul(
                sq[:, :], sx[:, TC * j:TC * (j + 1)], ps_rbc[:, 0:TC])))(),
                reads=[bx[j], b_psrbc], writes=[b_cacc[j % 2]])

        def ap_sub(j):
            sq = s_cacc[j % 2]
            p.add("vector", (lambda j=j, sq=sq, sxln=sxln: (lambda e: e.tensor_sub(
                sxln[:, TC * j:TC * (j + 1)], sq[:, :], ps_mrbc[:, 0:TC])))(),
                reads=[b_cacc[j % 2], b_psmrbc], writes=[bxln[j]])
        for j in range(0, NDM, 2):
            ap_mul(j); ap_mul(j + 1); ap_sub(j); ap_sub(j + 1)
            yield

    def ffn(q):
        sxln, bxln = s_xln[q], b_xln[q]
        sx, bx = s_x[q], b_x[q]
        banks2 = [(ps_xd, b_psxd), (ps_rbc, b_psrbc)]
        pend = []
        for j in range(NDM):
            w = j % 2
            dma(s_wfp[w][:, :].rearrange("p (k m) -> p k m", k=NDM),
                d_wft[:, :, j, :].transpose([1, 0, 2]),
                [b_wfp[w], b_wft[j]], [b_wfp[w]], sl_wfp[w])
            ps, bp = banks2[j % 2]
            for k in range(NDM):
                p.add("tensor", (lambda k=k, ps=ps, w=w, sxln=sxln: (lambda e: e.matmul(
                    ps[:, 0:TC], s_wfp[w][:, 128 * k:128 * (k + 1)],
                    sxln[:, TC * k:TC * (k + 1)],
                    start=(k == 0), stop=(k == NDM - 1))))(),
                    reads=[b_wfp[w], bxln[k]], writes=[bp])
            pend.append((lambda j=j, ps=ps, bp=bp: p.add(
                "vector", (lambda j=j, ps=ps, sx=sx: (lambda e: e.scalar_tensor_tensor(
                    sx[:, TC * j:TC * (j + 1)], ps[:, 0:TC], s_thr[:, 0:1],
                    sx[:, TC * j:TC * (j + 1)], ALU.mult, ALU.add)))(),
                reads=[bp, b_thr, bx[j]], writes=[bx[j]])))
            if len(pend) > 1:
                pend.pop(0)()
            yield
        while pend:
            pend.pop(0)()

    def front(c):
        q = c % 2
        t0 = c * TC
        sx, sxln = s_x[q], s_xln[q]
        bx, bxln = b_x[q], b_xln[q]
        for k in range(NTB):
            dma(s_tst[0][:, :], d_x[t0 + 128 * k:t0 + 128 * (k + 1), :],
                [b_tst[0]], [b_tst[0]], sl_xin[0])
            for j in range(NDM):
                ps, bp = (ps_a, b_psa) if j < 4 else (ps_b, b_psb)
                p.add("tensor", (lambda j=j, k=k, ps=ps: (lambda e: e.transpose(
                    ps[:, 128 * (j % 4):128 * (j % 4 + 1)],
                    s_tst[0][:, 128 * j:128 * (j + 1)],
                    s_ident[:, :])))(),
                    reads=[b_tst[0], b_ident], writes=[bp])
            for half in range(2):
                ps, bp = (ps_a, b_psa) if half == 0 else (ps_b, b_psb)
                dst = sx[:, :].rearrange("p (j m) -> p j m", j=NDM)[
                    :, 4 * half:4 * half + 4, 128 * k:128 * (k + 1)]
                p.add("scalar", (lambda ps=ps, dst=dst: (lambda e: e.copy(
                    dst, ps[:, 0:512].rearrange("p (j m) -> p j m", j=4))))(),
                    reads=[bp], writes=[bx[4 * half + jj] for jj in range(4)])
            yield
        yield from layernorm(q)
        ps_xp, bp_xp = rot()
        for k in range(NDM):
            p.add("tensor", (lambda k=k, ps=ps_xp, sxln=sxln: (lambda e: e.matmul(
                ps[:, 0:TC], s_wa[:, 128 * k:128 * (k + 1)], sxln[:, TC * k:TC * (k + 1)],
                start=(k == 0), stop=(k == NDM - 1))))(),
                reads=[b_wa, bxln[k]], writes=[bp_xp])
        ha, bha = s_ha[q], b_ha[q]
        if c == 0:
            p.add("vector", (lambda ha=ha, ps=ps_xp: (lambda e: e.tensor_tensor_scan(
                ha[:, :], s_Aabc[:, :], ps[:, 0:TC], 0.0, ALU.mult, ALU.add)))(),
                reads=[b_Aabc, bp_xp], writes=[bha])
        else:
            hprev = s_ha[1 - q]
            p.add("vector", (lambda ha=ha, ps=ps_xp, hp=hprev: (lambda e: e.tensor_tensor_scan(
                ha[:, :], s_Aabc[:, :], ps[:, 0:TC], hp[:, TC - 1:TC], ALU.mult, ALU.add)))(),
                reads=[b_Aabc, bp_xp, b_ha[1 - q]], writes=[bha])
        p.add("scalar", (lambda ha=ha: (lambda e: e.copy(s_habf[:, :], ha[:, :])))(),
              reads=[bha], writes=[b_habf])
        yield
        for j in range(NDM):
            ps, bp = rot()
            p.add("tensor", (lambda j=j, ps=ps: (lambda e: e.matmul(
                ps[:, 0:TC], s_ca[:, 128 * j:128 * (j + 1)], s_habf[:, :],
                start=True, stop=True)))(),
                reads=[b_ca, b_habf], writes=[bp])
            p.add("vector", (lambda j=j, ps=ps, sx=sx: (lambda e: e.tensor_add(
                sx[:, TC * j:TC * (j + 1)], sx[:, TC * j:TC * (j + 1)], ps[:, 0:TC])))(),
                reads=[bx[j], bp], writes=[bx[j]])
            p.add("vector", (lambda j=j, sx=sx, sxln=sxln: (lambda e: e.scalar_tensor_tensor(
                sx[:, TC * j:TC * (j + 1)], sxln[:, TC * j:TC * (j + 1)],
                s_Da[:, j:j + 1], sx[:, TC * j:TC * (j + 1)], ALU.mult, ALU.add)))(),
                reads=[bxln[j], b_Da, bx[j]], writes=[bx[j]])
            if j % 2 == 1:
                yield
        yield from layernorm(q)
        yield from ffn(q)
        yield from layernorm(q)

    def heavy(c):
        q = c % 2
        sxln, bxln = s_xln[q], b_xln[q]
        pending_xd = []
        for jp in range(NDI // 2):
            pair = (2 * jp, 2 * jp + 1)
            for j in pair:
                dma(s_dg[j % 2][:, :], d_diag[j, :, :],
                    [b_dg[j % 2], b_diag], [b_dg[j % 2]], sl_dg)

            def c_hin(j):
                xb = s_xbs[j % 2]
                p.add("vector", (lambda j=j, xb=xb: (lambda e: e.tensor_copy(
                    xb[:, 0:3], s_halo[:, 3 * j:3 * (j + 1)])))(),
                    reads=[b_halo[j]], writes=[b_xbs[j % 2]])

            def c_hout(j):
                xb = s_xbs[j % 2]
                p.add("vector", (lambda j=j, xb=xb: (lambda e: e.tensor_copy(
                    s_halo[:, 3 * j:3 * (j + 1)], xb[:, TC:TC + 3])))(),
                    reads=[b_xbs[j % 2]], writes=[b_halo[j]])

            for j in pair:
                w = j % 2
                dma(s_wfp[w][:, :].rearrange("p (k m) -> p k m", k=NDM),
                    d_wint[:, :, j, :].transpose([1, 0, 2]),
                    [b_wfp[w], b_wint[j]], [b_wfp[w]], sl_wfp[w])
                c_hin(j)
                ps, bp = rot()
                for k in range(NDM):
                    p.add("tensor", (lambda k=k, ps=ps, w=w, sxln=sxln: (lambda e: e.matmul(
                        ps[:, 0:TC], s_wfp[w][:, 128 * k:128 * (k + 1)],
                        sxln[:, TC * k:TC * (k + 1)],
                        start=(k == 0), stop=(k == NDM - 1))))(),
                        reads=[b_wfp[w], bxln[k]], writes=[bp])
                xb = s_xbs[j % 2]
                p.add("scalar", (lambda ps=ps, xb=xb: (lambda e: e.copy(
                    xb[:, 3:TC + 3], ps[:, 0:TC])))(),
                    reads=[bp], writes=[b_xbs[j % 2]])
                yield

            def c_conv(j):
                xb = s_xbs[j % 2]
                ca = s_cacc if j % 2 == 0 else s_cacc2
                bca = b_cacc if j % 2 == 0 else b_cacc2
                ps, bp = rot()
                for k in range(4):
                    p.add("tensor", (lambda j=j, k=k, ps=ps, xb=xb: (lambda e: e.matmul(
                        ps[:, 0:TC], s_dg[j % 2][:, 128 * k:128 * (k + 1)],
                        xb[:, k:TC + k],
                        start=(k == 0), stop=(k == 3))))(),
                        reads=[b_dg[j % 2], b_xbs[j % 2]], writes=[bp])
                p.add("scalar", (lambda j=j, ps=ps, ca=ca: (lambda e: e.activation(
                    ca[0][:, :], ps[:, 0:TC], AF.Tanh, bias=s_cbh[:, j:j + 1],
                    scale=s_konst[:, 2:3])))(),
                    reads=[bp, b_cbh, b_konst], writes=[bca[0]])
                p.add("scalar", (lambda j=j, ps=ps, ca=ca: (lambda e: e.activation(
                    ca[1][:, :], ps[:, 0:TC], AF.Identity, scale=s_konst[:, 2:3],
                    bias=s_cbh[:, j:j + 1])))(),
                    reads=[bp, b_cbh, b_konst], writes=[bca[1]])

            def c_xc(j):
                sxc = s_xc[0]
                ca = s_cacc if j % 2 == 0 else s_cacc2
                bca = b_cacc if j % 2 == 0 else b_cacc2
                p.add("vector", (lambda j=j, sxc=sxc, ca=ca: (lambda e: e.scalar_tensor_tensor(
                    sxc[:, TC * j:TC * (j + 1)], ca[0][:, :], 1.0, ca[1][:, :],
                    ALU.add, ALU.mult)))(),
                    reads=[bca[0], bca[1]], writes=[b_xc[0][j]])
            j0, j1 = pair
            c_conv(j0)
            yield
            c_xc(j0); c_hout(j0)
            c_conv(j1)
            yield
            c_xc(j1); c_hout(j1)
            if pending_xd:
                pending_xd.pop(0)()

            def em_xd(pair=pair):
                for j in pair:
                    p.add("tensor", (lambda k=j, q=q: (lambda e: e.matmul(
                        ps_xd[0:96, 0:TC], s_wx[:, 96 * k:96 * (k + 1)],
                        s_xc[0][:, TC * k:TC * (k + 1)],
                        start=(k == 0), stop=(k == NDI - 1))))(),
                        reads=[b_wx, b_xc[0][j]], writes=[b_psxd])
            pending_xd.append(em_xd)
            yield "p"
        pending_xd.pop(0)()
        p.add("scalar", (lambda q=q: (lambda e: e.copy(s_dlow[0][:, :], ps_xd[0:DTR, 0:TC])))(),
              reads=[b_psxd], writes=[b_dlow[0]])
        p.add("scalar", (lambda q=q: (lambda e: e.copy(s_bc[0][:, :], ps_xd[DTR:96, 0:TC])))(),
              reads=[b_psxd], writes=[b_bc[0]])
        dma(d_bcr[0], s_bc[0][:, :], [b_bc[0], b_bcr[0]], [b_bcr[0]], sl_bcr[0])

    def scan(c, bg, hbg, bgn):
        q = c % 2
        sxln, bxln = s_xln[q], b_xln[q]
        sxc = s_xc[0]
        syg, byg = s_yg[0], b_yg[0]

        bg_done = [False]

        def pull(k):
            for _ in range(k):
                if next(bg, StopIteration) is StopIteration:
                    bg_done[0] = True
                    return

        hp = [0]

        def pull_h(k, allowed):
            if not bg_done[0]:
                pull(k)
                return
            for _ in range(k):
                if hp[0] >= allowed:
                    return
                v = next(hbg, StopIteration)
                if v is StopIteration:
                    return
                if v == "p":
                    hp[0] += 1

        pending_v = []
        zok = [False]
        zbanks = [(ps_m, b_psm), (ps_q, b_psq), (ps_rbc, b_psrbc),
                  (ps_mrbc, b_psmrbc)]
        for jb in range(NDI // JB):
            js = list(range(JB * jb, JB * (jb + 1)))

            def zpre(jj, js=tuple(js)):
                j = js[jj]
                w = j % 2
                dma(s_win[w][:, 0:NDM * 128].rearrange("p (k m) -> p k m", k=NDM),
                    d_wint[:, :, NDI + j, :].transpose([1, 0, 2]),
                    [b_win[w], b_wint[NDI + j]], [b_win[w]], sl_win[w])
                ps, bp = zbanks[jj]
                for k in range(NDM):
                    p.add("tensor", (lambda k=k, ps=ps, w=w, sxln=sxln: (lambda e: e.matmul(
                        ps[:, 0:TC], s_win[w][:, 128 * k:128 * (k + 1)],
                        sxln[:, TC * k:TC * (k + 1)],
                        start=(k == 0), stop=(k == NDM - 1))))(),
                        reads=[b_win[w], bxln[k]], writes=[bp])

            dtb = [(ps_dt, b_psdt), (ps_m, b_psm)] if jb >= 1 else                   [(ps_dt, b_psdt), (ps_dt, b_psdt)]

            def dt_mm(jj):
                j = js[jj]
                ps, bp = dtb[jj % 2]
                p.add("tensor", (lambda j=j, q=q, ps=ps: (lambda e: e.matmul(
                    ps[:, 0:TC], s_wdt[:, 128 * j:128 * (j + 1)], s_dlow[0][:, :],
                    start=True, stop=True)))(),
                    reads=[b_wdt, b_dlow[0]], writes=[bp])

            def dt_tanh(jj):
                j = js[jj]
                ps, bp = dtb[jj % 2]
                p.add("scalar", (lambda j=j, jj=jj, ps=ps: (lambda e: e.activation(
                    s_dt[jj][:, :], ps[:, 0:TC], AF.Tanh, bias=s_bdth[:, j:j + 1],
                    scale=s_konst[:, 2:3])))(),
                    reads=[bp, b_bdth, b_konst], writes=[b_dt[jj]])

            def dt_ts(jj):
                p.add("scalar", (lambda jj=jj: (lambda e: e.activation(
                    s_dt[jj][:, :], s_dt[jj][:, :], AF.Identity,
                    scale=s_konst[:, 3:4], bias=s_konst[:, 4:5])))(),
                    reads=[b_dt[jj], b_konst], writes=[b_dt[jj]])

            def dt_v(jj):
                j = js[jj]
                p.add("vector", (lambda j=j, jj=jj, sxc=sxc: (lambda e: e.tensor_mul(
                    s_v[jj][:, :], s_dt[jj][:, :], sxc[:, TC * j:TC * (j + 1)])))(),
                    reads=[b_dt[jj], b_xc[0][j]], writes=[b_v[jj]])
            dt_mm(0); dt_tanh(0); dt_mm(1); dt_ts(0); dt_tanh(1); dt_mm(2)
            dt_v(0); dt_ts(1); dt_tanh(2); dt_mm(3); dt_v(1); dt_ts(2)
            dt_tanh(3); dt_v(2); dt_ts(3); dt_v(3)
            for n in range(NST):
                np2 = n % 2
                dma(s_Bbc[np2][:, :],
                    bass.AP(d_bcr, TC * n, [[0, 128], [1, TC]]),
                    [b_bcr[0], b_Bbc[np2]], [b_Bbc[np2]], sl_bbc[np2])
                dma(s_Cbc[np2][:, :],
                    bass.AP(d_bcr, TC * (NST + n), [[0, 128], [1, TC]]),
                    [b_bcr[0], b_Cbc[np2]], [b_Cbc[np2]], sl_cbc[np2])

                def em_dA(jj, nn=None):
                    idx = NST * js[jj] + (n if nn is None else nn)
                    p.add("scalar", (lambda jj=jj, idx=idx: (lambda e: e.activation(
                        s_da[jj % 2][:, :], s_dt[jj][:, :], AF.Exp,
                        scale=s_Aneg[:, idx:idx + 1])))(),
                        reads=[b_dt[jj], b_Aneg], writes=[b_da[jj % 2]])

                def em_u(jj):
                    p.add("vector", (lambda jj=jj, np2=np2: (lambda e: e.tensor_mul(
                        s_u[jj % 2][:, :], s_v[jj][:, :], s_Bbc[np2][:, :])))(),
                        reads=[b_v[jj], b_Bbc[np2]], writes=[b_u[jj % 2]])

                def em_tts(jj):
                    h2 = jj % 2
                    idx = NST * js[jj] + n
                    if c == 0:
                        p.add("vector", (lambda jj=jj, h2=h2: (lambda e: e.tensor_tensor_scan(
                            s_h[h2][:, :], s_da[jj % 2][:, :], s_u[jj % 2][:, :], 0.0,
                            ALU.mult, ALU.add)))(),
                            reads=[b_da[jj % 2], b_u[jj % 2]], writes=[b_h[h2]])
                    else:
                        p.add("vector", (lambda jj=jj, h2=h2, idx=idx: (lambda e: e.tensor_tensor_scan(
                            s_h[h2][:, :], s_da[jj % 2][:, :], s_u[jj % 2][:, :],
                            s_Hcar[:, idx:idx + 1], ALU.mult, ALU.add)))(),
                            reads=[b_da[jj % 2], b_u[jj % 2], b_Hcar], writes=[b_h[h2]])

                def em_z(jj):
                    h2 = jj % 2
                    if n == 0:
                        p.add("vector", (lambda jj=jj, h2=h2, np2=np2: (lambda e: e.tensor_mul(
                            s_y[jj][:, :], s_h[h2][:, :], s_Cbc[np2][:, :])))(),
                            reads=[b_h[h2], b_Cbc[np2]], writes=[b_y[jj]])
                    else:
                        p.add("vector", (lambda jj=jj, h2=h2, np2=np2: (lambda e: e.tensor_mul(
                            s_zt[h2][:, :], s_h[h2][:, :], s_Cbc[np2][:, :])))(),
                            reads=[b_h[h2], b_Cbc[np2]], writes=[b_zt[h2]])

                def em_y(jj):
                    if n == 0:
                        return
                    h2 = jj % 2
                    p.add("vector", (lambda jj=jj, h2=h2: (lambda e: e.tensor_add(
                        s_y[jj][:, :], s_y[jj][:, :], s_zt[h2][:, :])))(),
                        reads=[b_zt[h2], b_y[jj]], writes=[b_y[jj]])

                def em_car(jj):
                    h2 = jj % 2
                    idx = NST * js[jj] + n
                    p.add("scalar", (lambda h2=h2, idx=idx: (lambda e: e.copy(
                        s_Hcar[:, idx:idx + 1], s_h[h2][:, TC - 1:TC])))(),
                        reads=[b_h[h2]], writes=[b_Hcar])
                if n == 0:
                    em_dA(0); em_dA(1)
                em_u(0); em_u(1); em_tts(0); em_dA(2); em_u(2); em_tts(1)
                pull(1)
                em_z(0); em_car(0); em_dA(3); em_u(3); em_tts(2)
                pull(1)
                em_z(1); em_car(1); em_y(0); em_tts(3)
                if pending_v:
                    pending_v.pop(0)()
                em_z(2); em_car(2); em_y(1)
                pull(1)
                pull_h(1, 2 * jb)
                em_z(3); em_car(3); em_y(2); em_y(3)
                if n + 1 < NST:
                    em_dA(0, n + 1); em_dA(1, n + 1)
                pull_h(1, 2 * jb)
                if n == NST - 4:
                    zok[0] = jb >= 1 or bg_done[0]
                if zok[0] and n >= NST - 4:
                    zpre(n - (NST - 4))
            for jj, j in enumerate(js):
                p.add("vector", (lambda j=j, jj=jj, sxc=sxc: (lambda e: e.scalar_tensor_tensor(
                    s_y[jj][:, :], sxc[:, TC * j:TC * (j + 1)], s_Db[:, j:j + 1],
                    s_y[jj][:, :], ALU.mult, ALU.add)))(),
                    reads=[b_xc[0][j], b_Db, b_y[jj]], writes=[b_y[jj]])
            for jj, j in enumerate(js):
                if zok[0]:
                    ps, bp = zbanks[jj]
                else:
                    w = j % 2
                    dma(s_win[w][:, 0:NDM * 128].rearrange("p (k m) -> p k m", k=NDM),
                        d_wint[:, :, NDI + j, :].transpose([1, 0, 2]),
                        [b_win[w], b_wint[NDI + j]], [b_win[w]], sl_win[w])
                    ps, bp = rot()
                    for k in range(NDM):
                        p.add("tensor", (lambda k=k, ps=ps, w=w, sxln=sxln: (lambda e: e.matmul(
                            ps[:, 0:TC], s_win[w][:, 128 * k:128 * (k + 1)],
                            sxln[:, TC * k:TC * (k + 1)],
                            start=(k == 0), stop=(k == NDM - 1))))(),
                            reads=[b_win[w], bxln[k]], writes=[bp])
                p.add("scalar", (lambda ps=ps, bp=bp, u2=jj % 2: (lambda e: e.activation(
                    s_zt[u2][:, :], ps[:, 0:TC], AF.Tanh, scale=s_konst[:, 2:3])))(),
                    reads=[bp, b_konst], writes=[b_zt[jj % 2]])
                p.add("vector", (lambda ps=ps, bp=bp, u2=jj % 2, jj=jj: (lambda e: e.scalar_tensor_tensor(
                    s_da[jj % 2][:, :], s_zt[u2][:, :], 1.0, ps[:, 0:TC], ALU.add, ALU.mult)))(),
                    reads=[b_zt[jj % 2], bp], writes=[b_da[jj % 2]])
                p.add("vector", (lambda j=j, jj=jj, syg=syg: (lambda e: e.scalar_tensor_tensor(
                    syg[:, TC * j:TC * (j + 1)], s_da[jj % 2][:, :], 0.5, s_y[jj][:, :],
                    ALU.mult, ALU.mult)))(),
                    reads=[b_da[jj % 2], b_y[jj]], writes=[byg[j]])
                pull(2)

            def wo_group(jb=jb, js=tuple(js)):
                def emit(f):
                    w = f % 2
                    dma(s_wo2[w][:, :].rearrange("p (k m) -> p k m", k=JB),
                        d_woutt[JB * jb:JB * (jb + 1), :, f, :].transpose([1, 0, 2]),
                        [b_wo2[w], b_woutt[f]], [b_wo2[w]], sl_wo2[w])
                    ps, bp = rot()
                    for kk, k in enumerate(js):
                        p.add("tensor", (lambda k=k, ps=ps, w=w, kk=kk, syg=syg: (lambda e: e.matmul(
                            ps[:, 0:TC], s_wo2[w][:, 128 * kk:128 * (kk + 1)],
                            syg[:, TC * k:TC * (k + 1)],
                            start=(kk == 0), stop=(kk == JB - 1))))(),
                            reads=[b_wo2[w], byg[k]], writes=[bp])
                    p.add("vector", (lambda f=f, ps=ps, q=q: (lambda e: e.tensor_add(
                        s_x[q][:, TC * f:TC * (f + 1)], s_x[q][:, TC * f:TC * (f + 1)],
                        ps[:, 0:TC])))(),
                        reads=[b_x[q][f], bp], writes=[b_x[q][f]])
                return [(lambda f=f: emit(f)) for f in range(NDM)]
            pending_v.extend(wo_group())
        while next(hbg, StopIteration) is not StopIteration:
            pass

        def bgn_pull(k):
            for _ in range(k):
                if next(bgn, StopIteration) is StopIteration:
                    return
        for i, fn in enumerate(pending_v):
            fn()
            if i == 3 or i == 7:
                bgn_pull(1)
        bgn_pull(9)

    def post(c):
        q = c % 2
        t0 = c * TC
        sx, bx = s_x[q], b_x[q]
        yield from layernorm(q)
        yield from ffn(q)
        for k in range(NTB):
            for j in range(NDM):
                ps, bp = (ps_a, b_psa) if j < 4 else (ps_b, b_psb)
                p.add("tensor", (lambda j=j, k=k, ps=ps, sx=sx: (lambda e: e.transpose(
                    ps[:, 128 * (j % 4):128 * (j % 4 + 1)],
                    sx[:, TC * j + 128 * k:TC * j + 128 * (k + 1)], s_ident[:, :])))(),
                    reads=[bx[j], b_ident], writes=[bp])
            p.add("scalar", (lambda k=k: (lambda e: e.copy(
                s_tst[0][:, 0:512], ps_a[:, 0:512])))(),
                reads=[b_psa], writes=[b_tst[0]])
            p.add("scalar", (lambda k=k: (lambda e: e.copy(
                s_tst[0][:, 512:DM], ps_b[:, 0:512])))(),
                reads=[b_psb], writes=[b_tst[0]])
            dma(d_y[t0 + 128 * k:t0 + 128 * (k + 1), :], s_tst[0][:, :],
                [b_tst[0]], [b_tst[0]], sl_xout[0])
            yield

    def empty_gen():
        return
        yield

    def chain2(g1, g2):
        for v in g1:
            yield v
        for v in g2:
            yield v

    def mk_bg(c):
        return chain2(post(c - 1) if c > 0 else empty_gen(),
                      front(c + 1) if c + 1 < NCH else empty_gen())

    iw = init_wio()
    fg = front(0)
    while True:
        a = next(iw, StopIteration)
        b = next(fg, StopIteration)
        if a is StopIteration and b is StopIteration:
            break
    for _ in heavy(0):
        pass
    bg = mk_bg(0)
    for c in range(NCH):
        bgn = mk_bg(c + 1) if c + 1 < NCH else empty_gen()
        hbg = heavy(c + 1) if c + 1 < NCH else empty_gen()
        scan(c, bg, hbg, bgn)
        for _ in bg:
            pass
        for _ in hbg:
            pass
        bg = bgn
    for _ in bg:
        pass
    for _ in post(NCH - 1):
        pass

    p.finalize(nc, stack)
    with stack:
        with nc.Block() as block:
            @block.sync
            def _(sync):
                p.emit_engine("sync", sync)
                sync.wait_ge(sl_xout[0].sem, sl_xout[0].count)

            @block.tensor
            def _(t):
                p.emit_engine("tensor", t)

            @block.scalar
            def _(sc):
                p.emit_engine("scalar", sc)

            @block.vector
            def _(v):
                p.emit_engine("vector", v)
    return nc


def _host_inputs(inputs, b, L):
    f32 = lambda a: np.ascontiguousarray(np.asarray(a, dtype=np.float32))
    x = f32(inputs["L0_attn_a_x"])[b, :L]
    return {
        "x": x,
        "WaT": f32(np.asarray(inputs["L0_attn_a_B_w"]).T),
        "CaT": f32(np.asarray(inputs["L0_attn_a_C_w"]).T),
        "la": f32(np.asarray(inputs["L0_attn_a_log_A"]).reshape(128, 1)),
        "Da": f32(np.asarray(inputs["L0_attn_a_D_param"]).reshape(NDM, 128).T),
        "Win": f32(inputs["L0_attn_b_W_in"]),
        "cw": f32(np.asarray(inputs["L0_attn_b_conv_w"]).reshape(DI, 4)
                  .reshape(NDI, 128, 4).transpose(1, 0, 2).reshape(128, 4 * NDI)),
        "cb": f32(np.asarray(inputs["L0_attn_b_conv_b"]).reshape(NDI, 128).T),
        "Wx": f32(inputs["L0_attn_b_W_x"]),
        "Wdt": f32(inputs["L0_attn_b_W_dt"]),
        "bdt": f32(np.asarray(inputs["L0_attn_b_b_dt"]).reshape(NDI, 128).T),
        "Alog": f32(np.asarray(inputs["L0_attn_b_A_log"]).reshape(NDI, 128, NST)
                    .transpose(1, 0, 2).reshape(128, NDI * NST)),
        "Db": f32(np.asarray(inputs["L0_attn_b_D_param"]).reshape(NDI, 128).T),
        "Wout": f32(inputs["L0_attn_b_W_out"]),
        "WfT": f32(np.asarray(inputs["L0_ffn_weight"]).T),
        "ident": np.eye(128, dtype=np.float32),
        "onesred": np.stack([np.full(128, 1.0 / DM, np.float32),
                             np.ones(128, np.float32)], 1),
        "onesrow": np.ones((1, 128), np.float32),
    }


def kernel(**inputs):
    xfull = np.asarray(inputs["L0_attn_a_x"])
    nb, L = int(xfull.shape[0]), int(xfull.shape[1])
    nc = build_nc(L)
    in_maps = [_host_inputs(inputs, b, L) for b in range(nb)]
    res = run_bass_kernel_spmd(nc, in_maps, core_ids=list(range(nb)))
    out = np.stack([np.asarray(r["y"]) for r in res.results], axis=0)
    return out.astype(np.float32)
